# version 01
# baseline (speedup 1.0000x reference)
import numpy as np
from contextlib import ExitStack
from collections import defaultdict

import concourse.bass as bass
import concourse.mybir as mybir
from concourse.bass_utils import run_bass_kernel_spmd

F32 = mybir.dt.float32
BF16 = mybir.dt.bfloat16
I32 = mybir.dt.int32
AF = mybir.ActivationFunctionType
ALU = mybir.AluOpType
AX = mybir.AxisListType

NCORES = 8
S = 8192
D = 2048
NB = 16
BLK = S // NB
TLOC = 2 * BLK
NT = TLOC // 128
H = 16
DE = 8192
LKV = 512
INTER = 5504
EPS = 1e-5
QKS = float(1.0 / np.sqrt(128.0))
SHIFT0 = 16.0
SHIFT1 = 4.0

ENGINES = ("sync", "tensor", "vector", "scalar", "gpsimd")


def core_blocks(c):
    return (c, NB - 1 - c)


def core_perm(c):
    b0, b1 = core_blocks(c)
    return np.concatenate([np.arange(b0 * BLK, (b0 + 1) * BLK),
                           np.arange(b1 * BLK, (b1 + 1) * BLK)])


PERMS = [core_perm(c) for c in range(NCORES)]


class W:

    def __init__(self, nc, active, eng, sems):
        self.nc = nc
        self.active = active
        self.eng = eng
        self.sems = sems
        self.cnt = defaultdict(int)

    def at(self, name):
        return self.cnt[name]

    def do(self, engname, fn, wait=(), inc=()):
        if engname == self.active:
            for nm, v in wait:
                if v > 0:
                    self.eng.wait_ge(self.sems[nm], v)
            ins = fn(self.eng)
            for nm, n in inc:
                assert ins is not None, "inc requires a real instruction"
                ins = ins.then_inc(self.sems[nm], n)
        for nm, n in inc:
            self.cnt[nm] += n

    def wait(self, engname, *pairs):
        self.do(engname, lambda e: None, wait=pairs)

    def snap(self, *names):
        return [(nm, self.cnt[nm]) for nm in names]


def build_blocked(nc, sems, body):
    with nc.Block() as blk:
        for name in ENGINES:
            def mk(name):
                def f(eng):
                    body(W(nc, name, eng, sems))
                return f
            getattr(blk, name)(mk(name))


def mk_sems(nc, ctx, names):
    return {nm: ctx.enter_context(nc.semaphore(nm)) for nm in names}


SEM_NAMES = ["dL", "dW", "dS", "pe", "ve", "se", "gp"]



def emit_consts(w, ident, cols):
    for i, (ap, val) in enumerate(cols):
        last = i == len(cols) - 1
        w.do("gpsimd", lambda e, ap=ap, val=val: e.memset(ap, val),
             inc=[("gp", 1)] if last else ())
    if not cols:
        w.do("gpsimd", lambda e: e.memset(ident[0:1, 0:1], 0.0), inc=[("gp", 1)])
    gp_consts = w.at("gp")
    w.do("gpsimd", lambda e: e.memset(ident[:, :], 0.0), inc=[("gp", 1)])
    w.do("gpsimd", lambda e: e.affine_select(
        out=ident[:, :], in_=ident[:, :], compare_op=ALU.not_equal,
        fill=1.0, base=0, pattern=[[-1, 128]], channel_multiplier=1),
        wait=[("gp", w.at("gp"))], inc=[("gp", 1)])
    gp_ident = w.at("gp")
    return gp_consts, gp_ident


def emit_stats9(w, src_fn, st, tt, ceps, gp_consts, extra_first_wait=(),
                sq_buf=None, sq_free=0, scale2=None, scalef=None):

    def c(i):
        return st[:, tt * 9 + i:tt * 9 + i + 1]

    def step(eng, fn):
        w.do(eng, fn, wait=[("ve", w.at("ve")), ("se", w.at("se"))],
             inc=[("ve" if eng == "vector" else "se", 1)])
    w.do("vector", lambda e: e.reduce_sum(c(0), src_fn(), axis=AX.X),
         wait=[("ve", w.at("ve")), ("se", w.at("se"))] + list(extra_first_wait),
         inc=[("ve", 1)])
    if sq_buf is not None:
        w.do("vector", lambda e: e.tensor_tensor(
            sq_buf[:, :], src_fn(), src_fn(), ALU.mult),
            wait=[("ve", sq_free)], inc=[("ve", 1)])
        step("vector", lambda e: e.reduce_sum(c(1), sq_buf[:, :], axis=AX.X))
    step("vector", lambda e: e.tensor_scalar_mul(c(2), c(0), 1.0 / D))
    step("vector", lambda e: e.tensor_scalar_mul(c(6), c(1), 1.0 / D))
    step("vector", lambda e: e.tensor_tensor(c(7), c(2), c(2), ALU.mult))
    step("vector", lambda e: e.tensor_tensor(c(3), c(6), c(7), ALU.subtract))
    if scale2 is not None:
        step("vector", lambda e: e.tensor_tensor(c(3), c(3), scale2, ALU.mult))
    w.do("scalar", lambda e: e.activation(c(4), c(3), AF.Ln, bias=ceps[:, :]),
         wait=[("ve", w.at("ve")), ("se", w.at("se")), ("gp", gp_consts)],
         inc=[("se", 1)])
    step("scalar", lambda e: e.activation(c(8), c(4), AF.Exp, scale=-0.5))
    if scalef is not None:
        step("vector", lambda e: e.tensor_tensor(c(8), c(8), scalef, ALU.mult))
    step("vector", lambda e: e.scalar_tensor_tensor(
        c(5), c(2), -1.0, c(8), ALU.mult, ALU.mult))
    return w.at("ve")


def emit_ln_lnT(w, x_hbm, bufs, gp_consts, gp_ident, keep_x=None,
                x_wait_extra=None):
    xst, lnt, sqscr = bufs["xst"], bufs["lnt"], bufs["sqscr"]
    st = bufs["stats"]
    ceps = bufs["ceps"]
    lnT = bufs["lnT"]
    ptT = bufs["ptT"]
    ident = bufs["ident"]
    nstat = 9
    x_free_at = [0] * 2
    lnt_free = [0] * 2
    sq_free = [0] * 2

    for tt in range(NT):
        b = tt % 2
        wl = [("se", x_free_at[b])]
        if x_wait_extra is not None:
            wl = wl + x_wait_extra
        w.do("sync", lambda e, tt=tt, b=b: e.dma_start(
            out=xst[b][:, :], in_=x_hbm[tt * 128:(tt + 1) * 128, :]),
            wait=wl, inc=[("dL", 16)])
        ld = w.at("dL")
        co = tt * nstat

        def c(i, tt=tt):
            return st[:, tt * nstat + i:tt * nstat + i + 1]
        if keep_x is not None:
            w.do("vector", lambda e, b=b, tt=tt: e.tensor_copy(
                keep_x[tt][:, :], xst[b][:, :]), wait=[("dL", ld)])
        ve_nb = emit_stats9(w, (lambda b=b: xst[b][:, :]), st, tt, ceps,
                            gp_consts, extra_first_wait=[("dL", ld)],
                            sq_buf=sqscr[b], sq_free=sq_free[b])
        sq_free[b] = w.at("ve")
        w.do("scalar", lambda e, b=b, tt=tt: e.activation(
            lnt[b][:, :], xst[b][:, :], AF.Identity,
            bias=c(5, tt), scale=c(8, tt)),
            wait=[("ve", ve_nb), ("pe", lnt_free[b])], inc=[("se", 1)])
        x_free_at[b] = w.at("se")
        se_ln = w.at("se")
        for j in range(16):
            pb = j % 2
            w.do("tensor", lambda e, b=b, j=j, pb=pb: e.transpose(
                ptT[pb][:, :], lnt[b][:, 128 * j:128 * (j + 1)], ident[:, :]),
                wait=[("se", se_ln), ("gp", gp_ident),
                      ("ve", w.at("ve"))],
                inc=[("pe", 1)])
            w.do("vector", lambda e, j=j, tt=tt, pb=pb: e.tensor_copy(
                lnT[:, j * TLOC + tt * 128: j * TLOC + (tt + 1) * 128],
                ptT[pb][:, :]), wait=[("pe", w.at("pe"))], inc=[("ve", 1)])
        lnt_free[b] = w.at("pe")


def load_cast_w(w, src_hbm, row0, col0, ncols, stage_f32, dst_bf, dst_off,
                stage_free=0, dst_wait=()):
    w.do("sync", lambda e: e.dma_start(
        out=stage_f32[:, 0:ncols],
        in_=src_hbm[row0:row0 + 128, col0:col0 + ncols]),
        wait=[("ve", stage_free)], inc=[("dW", 16)])
    w.do("vector", lambda e: e.tensor_copy(
        dst_bf[:, dst_off:dst_off + ncols], stage_f32[:, 0:ncols]),
        wait=[("dW", w.at("dW"))] + list(dst_wait), inc=[("ve", 1)])
    return w.at("ve")



def build_l0qkv(parts=7):
    nc = bass.Bass("TRN2", target_bir_lowering=False,
                   detect_race_conditions=False)
    x = nc.dram_tensor("x", [TLOC, D], F32, kind="ExternalInput").ap()
    wqkvT = nc.dram_tensor("wqkvT", [D, 3 * D], F32, kind="ExternalInput").ap()
    wupT_sh = nc.dram_tensor("wupT_sh", [256, DE], F32, kind="ExternalInput").ap()
    wgT_sh = nc.dram_tensor("wgT_sh", [256, DE], F32, kind="ExternalInput").ap()
    wdT_sh = nc.dram_tensor("wdT_sh", [1024, D], F32, kind="ExternalInput").ap()
    qT = nc.dram_tensor("qT", [D, TLOC], BF16, kind="ExternalOutput").ap()
    kT = nc.dram_tensor("kT", [D, TLOC], BF16, kind="ExternalOutput").ap()
    vhm = nc.dram_tensor("vhm", [TLOC, H * 129], BF16, kind="ExternalOutput").ap()
    absw = nc.dram_tensor("absw", [1, 4], F32, kind="ExternalOutput").ap()
    if parts & 8:
        d_lnT = nc.dram_tensor("d_lnT", [128, 16 * TLOC], BF16,
                               kind="ExternalOutput").ap()
        d_st = nc.dram_tensor("d_st", [128, 72], F32,
                              kind="ExternalOutput").ap()

    with ExitStack() as ctx:
        sb = lambda n_, s, d=F32: ctx.enter_context(nc.sbuf_tensor(n_, s, d))
        ps = lambda n_, s, d=F32: ctx.enter_context(nc.psum_tensor(n_, s, d))
        sems = mk_sems(nc, ctx, SEM_NAMES)

        ident = sb("ident", [128, 128], BF16)
        ceps = sb("ceps", [128, 1])
        ones = sb("ones", [128, 1])
        xst = [sb(f"xst{i}", [128, D]) for i in range(2)]
        lnt = [sb(f"lnt{i}", [128, D], BF16) for i in range(2)]
        sqscr = [sb(f"sqscr{i}", [128, D]) for i in range(2)]
        stats = sb("stats", [128, 9 * NT])
        lnT = sb("lnT", [128, 16 * TLOC], BF16)
        ptT = [ctx.enter_context(nc.psum_tensor(f"ptT{i}", [128, 128], BF16))
               for i in range(2)]
        wst = [sb(f"wst{i}", [128, 1536]) for i in range(2)]
        wqb = sb("wqb", [128, 16 * 1536], BF16)
        pq = [ps(f"pq{i}", [128, 512]) for i in range(2)]
        stg = [sb(f"stg{i}", [128, TLOC], BF16) for i in range(2)]
        vstage = sb("vstage", [128, TLOC], BF16)
        ptv = [ctx.enter_context(nc.psum_tensor(f"ptv{i}", [128, 128], BF16))
               for i in range(2)]
        vhm_sb = [sb(f"vhm{t}", [128, H * 129], BF16) for t in range(NT)]
        abs_st = [sb(f"abst{i}", [128, D]) for i in range(2)]
        abs_scr = sb("abscr", [128, D], BF16)
        acol = sb("acol", [128, 1])
        pabs = ps("pabs", [1, 4])
        absw_sb = sb("absw_sb", [1, 4])

        def body(w):
            gp_c, gp_i = emit_consts(w, ident, [(ceps[:, :], EPS),
                                                (ones[:, :], 1.0)])
            for t in range(NT):
                for j in range(H):
                    last = (t == NT - 1 and j == H - 1)
                    w.do("gpsimd", lambda e, t=t, j=j: e.memset(
                        vhm_sb[t][:, 129 * j + 128:129 * j + 129], 1.0),
                        inc=[("gp", 1)] if last else ())
            gp_vones = w.at("gp")

            emit_ln_lnT(w, x, dict(xst=xst, lnt=lnt, sqscr=sqscr, stats=stats,
                                   ceps=ceps, lnT=lnT, ptT=ptT, ident=ident),
                        gp_c, gp_i)
            if parts & 8:
                w.do("sync", lambda e: e.dma_start(out=d_lnT, in_=lnT[:, :]),
                     wait=[("ve", w.at("ve")), ("se", w.at("se"))],
                     inc=[("dS", 16)])
                w.do("sync", lambda e: e.dma_start(out=d_st, in_=stats[:, :]),
                     inc=[("dS", 16)])

            lnT_ready = w.at("ve")
            pq_free = [0, 0]
            stg_free = [0, 0]
            vhm_ev = 0
            cast_at = [0, 0]
            for q in range(4 if parts & 1 else 0):
                wqb_free = [("pe", w.at("pe"))]
                for i in range(16):
                    cast_at[i % 2] = load_cast_w(
                        w, wqkvT, 128 * i, 1536 * q, 1536,
                        wst[i % 2], wqb, i * 1536,
                        stage_free=cast_at[i % 2],
                        dst_wait=wqb_free if i < 2 else ())
                wq_ready = w.at("ve")
                for ocl in range(12):
                    oc = 12 * q + ocl
                    if not (parts & 1):
                        continue
                    if oc >= 32 and not (parts & 2):
                        continue
                    for th in range(2):
                        pb = (2 * ocl + th) % 2
                        for i in range(16):
                            w.do("tensor", lambda e, i=i, ocl=ocl, th=th, pb=pb:
                                 e.matmul(pq[pb][:, :],
                                          wqb[:, i * 1536 + 128 * ocl:
                                              i * 1536 + 128 * (ocl + 1)],
                                          lnT[:, i * TLOC + 512 * th:
                                              i * TLOC + 512 * (th + 1)],
                                          start=(i == 0), stop=(i == 15)),
                                 wait=([("ve", max(wq_ready, lnT_ready)),
                                        ("se", pq_free[pb])] if i == 0 else ()),
                                 inc=[("pe", 1)] if i == 15 else ())
                        pe_mm = w.at("pe")
                        if oc < 32:
                            sbuf_i = oc % 2
                            w.do("scalar", lambda e, th=th, pb=pb, sbuf_i=sbuf_i:
                                 e.copy(stg[sbuf_i][:, 512 * th:512 * (th + 1)],
                                        pq[pb][:, :]),
                                 wait=[("pe", pe_mm), ("dS", stg_free[sbuf_i])],
                                 inc=[("se", 1)])
                        else:
                            w.do("scalar", lambda e, th=th, pb=pb:
                                 e.copy(vstage[:, 512 * th:512 * (th + 1)],
                                        pq[pb][:, :]),
                                 wait=[("pe", pe_mm)],
                                 inc=[("se", 1)])
                        pq_free[pb] = w.at("se")
                    if oc < 32:
                        sbuf_i = oc % 2
                        dst = qT if oc < 16 else kT
                        r0 = 128 * (oc % 16)
                        w.do("sync", lambda e, dst=dst, r0=r0, sbuf_i=sbuf_i:
                             e.dma_start(out=dst[r0:r0 + 128, :],
                                         in_=stg[sbuf_i][:, :]),
                             wait=[("se", w.at("se"))], inc=[("dS", 16)])
                        stg_free[sbuf_i] = w.at("dS")
                    else:
                        j = oc - 32
                        se_v = w.at("se")
                        for tt in range(NT):
                            pb = tt % 2
                            w.do("tensor", lambda e, tt=tt, pb=pb: e.transpose(
                                ptv[pb][:, :],
                                vstage[:, 128 * tt:128 * (tt + 1)], ident[:, :]),
                                wait=[("se", se_v), ("ve", w.at("ve")),
                                      ("gp", gp_i)],
                                inc=[("pe", 1)])
                            w.do("vector", lambda e, tt=tt, j=j, pb=pb:
                                 e.tensor_copy(
                                     vhm_sb[tt][:, 129 * j:129 * j + 128],
                                     ptv[pb][:, :]),
                                 wait=[("pe", w.at("pe"))], inc=[("ve", 1)])
                        vhm_ev = w.at("ve")
            if parts & 2:
                for tt in range(NT):
                    w.do("sync", lambda e, tt=tt: e.dma_start(
                        out=vhm[128 * tt:128 * (tt + 1), :],
                        in_=vhm_sb[tt][:, :]),
                        wait=[("ve", w.at("ve")), ("gp", gp_vones)],
                        inc=[("dS", 16)])

            chunks = ([(wupT_sh, r, c4, 0) for r in range(2) for c4 in range(4)]
                      + [(wgT_sh, r, c4, 1) for r in range(2) for c4 in range(4)]
                      + [(wdT_sh, r, 0, 2) for r in range(8)])
            nchunk = [8, 8, 8]
            seen = [0, 0, 0]
            abs_free = [0, 0]
            k = 0
            if not (parts & 4):
                chunks = []
            for src, r, c4, m in chunks:
                b = k % 2
                k += 1
                w.do("sync", lambda e, src=src, r=r, c4=c4, b=b: e.dma_start(
                    out=abs_st[b][:, :],
                    in_=src[128 * r:128 * (r + 1), D * c4:D * (c4 + 1)]),
                    wait=[("se", abs_free[b])], inc=[("dW", 16)])
                w.do("scalar", lambda e, b=b: e.activation(
                    abs_scr[:, :], abs_st[b][:, :], AF.Abs, accum_out=acol[:, :]),
                    wait=[("dW", w.at("dW")), ("pe", w.at("pe"))],
                    inc=[("se", 1)])
                abs_free[b] = w.at("se")
                w.do("tensor", lambda e, m=m, fi=(seen[m] == 0),
                     la=(seen[m] == nchunk[m] - 1): e.matmul(
                    pabs[0:1, 0:1], acol[:, :], ones[:, :],
                    start=fi, stop=la),
                    wait=[("se", w.at("se")), ("gp", gp_c),
                          ("ve", w.at("ve") if seen[m] == 0 else 0)],
                    inc=[("pe", 1)])
                seen[m] += 1
                if seen[m] == nchunk[m]:
                    w.do("vector", lambda e, m=m: e.tensor_copy(
                        absw_sb[0:1, m:m + 1], pabs[0:1, 0:1]),
                        wait=[("pe", w.at("pe"))], inc=[("ve", 1)])
            if parts & 4:
                w.do("vector", lambda e: e.memset(absw_sb[0:1, 3:4], 0.0),
                     inc=[("ve", 1)])
                w.do("sync", lambda e: e.dma_start(out=absw, in_=absw_sb[:, :]),
                     wait=[("ve", w.at("ve"))], inc=[("dS", 16)])
            w.wait("sync", ("dS", w.at("dS")))

        build_blocked(nc, sems, body)
    return nc



def emit_postln_residual(w, src_tiles, x_hbm, out_hbm, st2, ceps, gp_consts,
                         sqscr2, bufs, src_ready_fn, scale2=None, scalef=None,
                         x_extra_wait=()):
    xst2 = bufs["xst2"]
    t1 = bufs["t1"]
    x1t = bufs["x1t"]
    x_free = [0, 0]
    sqf = 0
    for tt in range(NT):
        b = tt % 2
        w.do("sync", lambda e, tt=tt, b=b: e.dma_start(
            out=xst2[b][:, :], in_=x_hbm[tt * 128:(tt + 1) * 128, :]),
            wait=[("ve", x_free[b])] + list(x_extra_wait), inc=[("dL", 16)])
        ld = w.at("dL")
        emit_stats9(w, (lambda tt=tt: src_tiles[tt][:, :]), st2, tt, ceps,
                    gp_consts, extra_first_wait=list(src_ready_fn(tt)),
                    sq_buf=sqscr2, sq_free=sqf, scale2=scale2, scalef=scalef)
        sqf = w.at("ve")

        def c(i, tt=tt):
            return st2[:, tt * 9 + i:tt * 9 + i + 1]
        w.do("scalar", lambda e, tt=tt: e.activation(
            t1[:, :], src_tiles[tt][:, :], AF.Identity,
            bias=c(5), scale=c(8)),
            wait=[("ve", w.at("ve")), ("se", w.at("se"))], inc=[("se", 1)])
        w.do("vector", lambda e, tt=tt, b=b: e.tensor_add(
            x1t[:, :], t1[:, :], xst2[b][:, :]),
            wait=[("se", w.at("se")), ("dL", ld), ("dS", w.at("dS"))],
            inc=[("ve", 1)])
        x_free[b] = w.at("ve")
        w.do("sync", lambda e, tt=tt: e.dma_start(
            out=out_hbm[tt * 128:(tt + 1) * 128, :], in_=x1t[:, :]),
            wait=[("ve", w.at("ve"))], inc=[("dS", 16)])


def build_l0attn():
    nc = bass.Bass("TRN2", target_bir_lowering=False,
                   detect_race_conditions=False)
    x = nc.dram_tensor("x", [TLOC, D], F32, kind="ExternalInput").ap()
    qT = nc.dram_tensor("qT", [D, TLOC], BF16, kind="ExternalInput").ap()
    kTf = nc.dram_tensor("kTf", [D, S], BF16, kind="ExternalInput").ap()
    vhmf = nc.dram_tensor("vhmf", [H, S, 129], BF16, kind="ExternalInput").ap()
    woT = nc.dram_tensor("woT", [D, D], F32, kind="ExternalInput").ap()
    x1 = nc.dram_tensor("x1", [TLOC, D], F32, kind="ExternalOutput").ap()

    with ExitStack() as ctx:
        sb = lambda n_, s, d=F32: ctx.enter_context(nc.sbuf_tensor(n_, s, d))
        ps = lambda n_, s, d=F32: ctx.enter_context(nc.psum_tensor(n_, s, d))
        sems = mk_sems(nc, ctx, SEM_NAMES)

        ident = sb("ident", [128, 128], BF16)
        ceps = sb("ceps", [128, 1])
        cshift = sb("cshift", [128, 1])
        shared = sb("shared", [128, 33024], BF16)
        qT_sb = shared[:, 0:H * TLOC]
        kh = shared[:, 16384:16384 + S]
        vh = shared[:, 24576:24576 + 64 * 129]
        woT_view = shared[:, 0:16 * D]
        ptb = [sb(f"ptb{i}", [128, 512], BF16) for i in range(2)]
        rq = sb("rq", [128, 4])
        ostg = sb("ostg", [128, 512], BF16)
        oT = sb("oT", [128, H * TLOC], BF16)
        wstg = [sb(f"wstg{i}", [128, D]) for i in range(2)]
        woT_sb = woT_view
        ao = sb("ao", [128, D])
        xst2 = [sb(f"xst2{i}", [128, D]) for i in range(2)]
        t1 = sb("t1", [128, D])
        x1t = sb("x1t", [128, D])
        sqscr2 = sb("sqscr2", [128, D])
        st2 = sb("st2", [128, 9 * NT])
        psS = [ps(f"psS{i}", [128, 512]) for i in range(2)]
        psO = [ps(f"psO{i}", [128, 129]) for i in range(4)]
        ptv = [ctx.enter_context(nc.psum_tensor(f"ptv{i}", [128, 128], BF16))
               for i in range(2)]
        po = psS

        def body(w):
            gp_c, gp_i = emit_consts(w, ident, [
                (ceps[:, :], EPS), (cshift[:, :], -SHIFT0)])
            for g in range(4):
                w.do("sync", lambda e, g=g: e.dma_start(
                    out=qT_sb.rearrange("p (h t) -> p h t", h=H)[:, 4 * g:4 * (g + 1), :],
                    in_=qT.rearrange("(h p) t -> p h t", p=128)[:, 4 * g:4 * (g + 1), :]),
                    inc=[("dL", 16)])
            qt_ld = w.at("dL")
            kh_free = 0
            vh_free = 0
            ptb_free = [0, 0]
            psS_free = [0, 0]
            psO_free = 0
            for h in range(H):
                w.do("sync", lambda e, h=h: e.dma_start(
                    out=kh[:, :], in_=kTf[128 * h:128 * (h + 1), :]),
                    wait=[("pe", kh_free)], inc=[("dW", 16)])
                kh_ld = w.at("dW")
                for g in range(4):
                    w.do("sync", lambda e, h=h, g=g: e.dma_start(
                        out=vh.rearrange("p (st c) -> p st c", c=129)[:, 16 * g:16 * (g + 1), :],
                        in_=vhmf[h].rearrange("(st p) c -> p st c", p=128)[:, 16 * g:16 * (g + 1), :]),
                        wait=[("pe", vh_free)] if g == 0 else (),
                        inc=[("dW", 16)])
                vh_ld = w.at("dW")
                for qg in range(2):
                    first_av = True
                    for stt in range(64):
                        sb_i = stt % 2
                        w.do("tensor", lambda e, h=h, qg=qg, stt=stt, sb_i=sb_i:
                             e.matmul(psS[sb_i][:, :],
                                      kh[:, 128 * stt:128 * (stt + 1)],
                                      qT_sb[:, h * TLOC + 512 * qg:
                                            h * TLOC + 512 * (qg + 1)],
                                      start=True, stop=True),
                             wait=[("dW", kh_ld), ("dL", qt_ld),
                                   ("se", psS_free[sb_i])],
                             inc=[("pe", 1)])
                        pe_s = w.at("pe")
                        w.do("scalar", lambda e, sb_i=sb_i: e.activation(
                            ptb[sb_i][:, :], psS[sb_i][:, :], AF.Exp,
                            scale=QKS, bias=cshift[:, :]),
                            wait=[("pe", pe_s), ("gp", gp_c),
                                  ("pe", ptb_free[sb_i])],
                            inc=[("se", 1)])
                        psS_free[sb_i] = w.at("se")
                        se_exp = w.at("se")
                        for j in range(4):
                            w.do("tensor", lambda e, j=j, stt=stt, sb_i=sb_i,
                                 fa=first_av: e.matmul(
                                     psO[j][:, :],
                                     ptb[sb_i][:, 128 * j:128 * (j + 1)],
                                     vh[:, 129 * stt:129 * (stt + 1)],
                                     start=fa, stop=(stt == 63)),
                                 wait=([("se", se_exp), ("dW", vh_ld),
                                        ("se", psO_free)] if j == 0 else ()),
                                 inc=[("pe", 1)] if j == 3 else ())
                        first_av = False
                        ptb_free[sb_i] = w.at("pe")
                    pe_av = w.at("pe")
                    for j in range(4):
                        w.do("vector", lambda e, j=j: e.reciprocal(
                            rq[:, j:j + 1], psO[j][:, 128:129]),
                            wait=[("pe", pe_av), ("ve", w.at("ve")),
                                  ("se", w.at("se"))], inc=[("ve", 1)])
                        w.do("scalar", lambda e, j=j: e.activation(
                            ostg[:, 128 * j:128 * (j + 1)],
                            psO[j][:, 0:128],
                            AF.Identity, scale=rq[:, j:j + 1]),
                            wait=[("ve", w.at("ve")), ("pe", w.at("pe"))],
                            inc=[("se", 1)])
                    psO_free = w.at("se")
                    se_of = w.at("se")
                    for j in range(4):
                        pb = j % 2
                        w.do("tensor", lambda e, j=j, pb=pb: e.transpose(
                            ptv[pb][:, :], ostg[:, 128 * j:128 * (j + 1)],
                            ident[:, :]),
                            wait=[("se", se_of), ("gp", gp_i),
                                  ("ve", w.at("ve"))],
                            inc=[("pe", 1)])
                        w.do("vector", lambda e, j=j, h=h, qg=qg, pb=pb:
                             e.tensor_copy(
                                 oT[:, h * TLOC + qg * 512 + 128 * j:
                                    h * TLOC + qg * 512 + 128 * (j + 1)],
                                 ptv[pb][:, :]),
                             wait=[("pe", w.at("pe"))], inc=[("ve", 1)])
                kh_free = w.at("pe")
                vh_free = w.at("pe")
            oT_ready = w.at("ve")

            attn_pe = w.at("pe")
            cast_at = [0, 0]
            for i in range(16):
                w.do("sync", lambda e, i=i: e.dma_start(
                    out=wstg[i % 2][:, :], in_=woT[128 * i:128 * (i + 1), :]),
                    wait=[("ve", cast_at[i % 2])], inc=[("dW", 16)])
                w.do("vector", lambda e, i=i: e.tensor_copy(
                    woT_sb[:, i * D:(i + 1) * D], wstg[i % 2][:, :]),
                    wait=[("dW", w.at("dW"))] + ([("pe", attn_pe),
                                                  ("se", w.at("se"))]
                                                 if i == 0 else []),
                    inc=[("ve", 1)])
                cast_at[i % 2] = w.at("ve")
            wo_ready = w.at("ve")

            po_free = list(psS_free)
            x_free = [0, 0]
            sqf = 0
            for tt in range(NT):
                b = tt % 2
                for dc in range(4):
                    pb = dc % 2
                    for hc in range(16):
                        w.do("tensor", lambda e, tt=tt, dc=dc, hc=hc, pb=pb:
                             e.matmul(po[pb][:, :],
                                      oT[:, hc * TLOC + 128 * tt:
                                         hc * TLOC + 128 * (tt + 1)],
                                      woT_sb[:, hc * D + 512 * dc:
                                             hc * D + 512 * (dc + 1)],
                                      start=(hc == 0), stop=(hc == 15)),
                             wait=([("ve", max(wo_ready, oT_ready)),
                                    ("se", po_free[pb])] if hc == 0 else ()),
                             inc=[("pe", 1)] if hc == 15 else ())
                    w.do("scalar", lambda e, tt=tt, dc=dc, pb=pb: e.copy(
                        ao[:, 512 * dc:512 * (dc + 1)], po[pb][:, :]),
                        wait=[("pe", w.at("pe")), ("ve", w.at("ve"))],
                        inc=[("se", 1)])
                    po_free[pb] = w.at("se")
                ao_se = w.at("se")
                w.do("sync", lambda e, tt=tt, b=b: e.dma_start(
                    out=xst2[b][:, :], in_=x[tt * 128:(tt + 1) * 128, :]),
                    wait=[("ve", x_free[b])], inc=[("dL", 16)])
                ld = w.at("dL")
                emit_stats9(w, (lambda: ao[:, :]), st2, tt, ceps, gp_c,
                            extra_first_wait=[("se", ao_se)],
                            sq_buf=sqscr2, sq_free=sqf)
                sqf = w.at("ve")

                def c(i, tt=tt):
                    return st2[:, tt * 9 + i:tt * 9 + i + 1]
                w.do("scalar", lambda e, tt=tt: e.activation(
                    t1[:, :], ao[:, :], AF.Identity, bias=c(5), scale=c(8)),
                    wait=[("ve", w.at("ve")), ("se", w.at("se"))],
                    inc=[("se", 1)])
                w.do("vector", lambda e, b=b: e.tensor_add(
                    x1t[:, :], t1[:, :], xst2[b][:, :]),
                    wait=[("se", w.at("se")), ("dL", ld), ("dS", w.at("dS"))],
                    inc=[("ve", 1)])
                x_free[b] = w.at("ve")
                w.do("sync", lambda e, tt=tt: e.dma_start(
                    out=x1[tt * 128:(tt + 1) * 128, :], in_=x1t[:, :]),
                    wait=[("ve", w.at("ve"))], inc=[("dS", 16)])
            w.wait("sync", ("dS", w.at("dS")))

        build_blocked(nc, sems, body)
    return nc



def build_l0quant():
    nc = bass.Bass("TRN2", target_bir_lowering=False,
                   detect_race_conditions=False)
    absw_all = nc.dram_tensor("absw_all", [NCORES, 4], F32,
                              kind="ExternalInput").ap()
    wupT_sh = nc.dram_tensor("wupT_sh", [256, DE], F32, kind="ExternalInput").ap()
    wgT_sh = nc.dram_tensor("wgT_sh", [256, DE], F32, kind="ExternalInput").ap()
    wdT_sh = nc.dram_tensor("wdT_sh", [1024, D], F32, kind="ExternalInput").ap()
    tupT_sh = nc.dram_tensor("tupT_sh", [256, DE], BF16, kind="ExternalOutput").ap()
    tgT_sh = nc.dram_tensor("tgT_sh", [256, DE], BF16, kind="ExternalOutput").ap()
    tdT_sh = nc.dram_tensor("tdT_sh", [1024, D], BF16, kind="ExternalOutput").ap()
    scl = nc.dram_tensor("scl", [1, 4], F32, kind="ExternalOutput").ap()

    with ExitStack() as ctx:
        sb = lambda n_, s, d=F32: ctx.enter_context(nc.sbuf_tensor(n_, s, d))
        ps = lambda n_, s, d=F32: ctx.enter_context(nc.psum_tensor(n_, s, d))
        sems = mk_sems(nc, ctx, SEM_NAMES)
        ones1 = sb("ones1", [1, 128])
        aw = sb("aw", [1, 32])
        tot = sb("tot", [1, 4])
        scl_sb = sb("scl_sb", [1, 4])
        thr1 = sb("thr1", [1, 4])
        thrb = sb("thrb", [128, 4])
        nthrb = sb("nthrb", [128, 4])
        pbc = ps("pbc", [128, 4])
        wst = [sb(f"wst{i}", [128, D]) for i in range(2)]
        sg1 = sb("sg1", [128, D])
        sg2 = sb("sg2", [128, D])
        tq = [sb(f"tq{i}", [128, D], BF16) for i in range(2)]

        def body(w):
            def step(eng, fn, extra=()):
                w.do(eng, fn, wait=[("ve", w.at("ve")), ("se", w.at("se"))]
                     + list(extra),
                     inc=[("ve" if eng == "vector" else "se", 1)])
            w.do("gpsimd", lambda e: e.memset(ones1[:, :], 1.0), inc=[("gp", 1)])
            w.do("sync", lambda e: e.dma_start(
                out=aw[:, :],
                in_=absw_all.rearrange("(o r) k -> o (r k)", o=1)),
                inc=[("dL", 16)])
            step("vector", lambda e: e.reduce_sum(
                tot[:, :], aw.rearrange("o (r k) -> o k r", r=NCORES),
                axis=AX.X), extra=[("dL", w.at("dL"))])
            step("vector", lambda e: e.tensor_scalar(
                scl_sb[:, :], tot[:, :], 1.0 / (D * DE), 1e-8,
                ALU.mult, ALU.add))
            step("vector", lambda e: e.tensor_scalar_mul(
                thr1[:, :], scl_sb[:, :], 0.5))
            w.do("tensor", lambda e: e.matmul(
                pbc[:, :], ones1[:, :], thr1[:, :], start=True, stop=True),
                wait=[("ve", w.at("ve")), ("gp", 1)], inc=[("pe", 1)])
            step("vector", lambda e: e.tensor_copy(thrb[:, :], pbc[:, :]),
                 extra=[("pe", w.at("pe"))])
            step("vector", lambda e: e.tensor_scalar_mul(
                nthrb[:, :], thrb[:, :], -1.0))
            w.do("sync", lambda e: e.dma_start(out=scl, in_=scl_sb[:, :]),
                 wait=[("ve", w.at("ve"))], inc=[("dS", 16)])
            thr_ve = w.at("ve")

            chunks = ([(wupT_sh, tupT_sh, r, c4, 0)
                       for r in range(2) for c4 in range(4)]
                      + [(wgT_sh, tgT_sh, r, c4, 1)
                         for r in range(2) for c4 in range(4)]
                      + [(wdT_sh, tdT_sh, r, 0, 2) for r in range(8)])
            wfree = [0, 0]
            tqfree = [0, 0]
            for k, (src, dst, r, c4, m) in enumerate(chunks):
                b = k % 2
                w.do("sync", lambda e, src=src, r=r, c4=c4, b=b: e.dma_start(
                    out=wst[b][:, :],
                    in_=src[128 * r:128 * (r + 1), D * c4:D * (c4 + 1)]),
                    wait=[("se", wfree[b])], inc=[("dW", 16)])
                ld = w.at("dW")
                w.do("scalar", lambda e, b=b, m=m: e.activation(
                    sg1[:, :], wst[b][:, :], AF.Sign, bias=nthrb[:, m:m + 1]),
                    wait=[("dW", ld), ("ve", max(thr_ve, w.at("ve"))),
                          ("se", w.at("se"))], inc=[("se", 1)])
                w.do("scalar", lambda e, b=b, m=m: e.activation(
                    sg2[:, :], wst[b][:, :], AF.Sign, bias=thrb[:, m:m + 1]),
                    wait=[("se", w.at("se"))], inc=[("se", 1)])
                wfree[b] = w.at("se")
                w.do("vector", lambda e, b=b: e.tensor_add(
                    tq[b][:, :], sg1[:, :], sg2[:, :]),
                    wait=[("se", w.at("se")), ("dS", tqfree[b]),
                          ("ve", w.at("ve"))], inc=[("ve", 1)])
                w.do("sync", lambda e, dst=dst, r=r, c4=c4, b=b: e.dma_start(
                    out=dst[128 * r:128 * (r + 1), D * c4:D * (c4 + 1)],
                    in_=tq[b][:, :]),
                    wait=[("ve", w.at("ve"))], inc=[("dS", 16)])
                tqfree[b] = w.at("dS")
            w.wait("sync", ("dS", w.at("dS")))

        build_blocked(nc, sems, body)
    return nc



def build_l0ffn():
    nc = bass.Bass("TRN2", target_bir_lowering=False,
                   detect_race_conditions=False)
    x1 = nc.dram_tensor("x1", [TLOC, D], F32, kind="ExternalInput").ap()
    tupT = nc.dram_tensor("tupT", [D, DE], BF16, kind="ExternalInput").ap()
    tgT = nc.dram_tensor("tgT", [D, DE], BF16, kind="ExternalInput").ap()
    tdT = nc.dram_tensor("tdT", [DE, D], BF16, kind="ExternalInput").ap()
    scl = nc.dram_tensor("scl", [1, 4], F32, kind="ExternalInput").ap()
    x2 = nc.dram_tensor("x2", [TLOC, D], F32, kind="ExternalOutput").ap()

    NGRP, GE = 16, 4
    with ExitStack() as ctx:
        sb = lambda n_, s, d=F32: ctx.enter_context(nc.sbuf_tensor(n_, s, d))
        ps = lambda n_, s, d=F32: ctx.enter_context(nc.psum_tensor(n_, s, d))
        sems = mk_sems(nc, ctx, SEM_NAMES)
        ident = sb("ident", [128, 128], BF16)
        ceps = sb("ceps", [128, 1])
        ones1 = sb("ones1", [1, 128])
        xst = [sb(f"xst{i}", [128, D]) for i in range(2)]
        lnt = [sb(f"lnt{i}", [128, D], BF16) for i in range(2)]
        sqscr = [sb(f"sqscr{i}", [128, D]) for i in range(2)]
        stats = sb("stats", [128, 9 * NT])
        lnT = sb("lnT", [128, 16 * TLOC], BF16)
        ptT = [ctx.enter_context(nc.psum_tensor(f"ptT{i}", [128, 128], BF16))
               for i in range(2)]
        scl_sb = sb("scl_sb", [1, 4])
        fb1 = sb("fb1", [1, 4])
        fb = sb("fb", [128, 4])
        upw = [sb(f"upw{i}", [128, 16 * 128], BF16) for i in range(2)]
        gw = [sb(f"gw{i}", [128, 16 * 128], BF16) for i in range(2)]
        sil = sb("sil", [128, 512])
        hT = sb("hT", [128, GE * TLOC], BF16)
        dwg = sb("dwg", [128, GE * D], BF16)
        acc = [sb(f"acc{t}", [128, D]) for t in range(NT)]
        psU = [ps(f"psU{i}", [128, 512]) for i in range(2)]
        psG = [ps(f"psG{i}", [128, 512]) for i in range(2)]
        psD = [ps(f"psD{i}", [128, 512]) for i in range(2)]
        pbc = psU[0][:, 0:4]
        t1 = sb("t1", [128, D])
        x1t = sb("x1t", [128, D])
        st2 = sb("st2", [128, 9 * NT])

        def body(w):
            gp_c, gp_i = emit_consts(w, ident, [(ceps[:, :], EPS)])
            w.do("gpsimd", lambda e: e.memset(ones1[:, :], 1.0), inc=[("gp", 1)])
            gp_ones = w.at("gp")

            def step(eng, fn, extra=()):
                w.do(eng, fn, wait=[("ve", w.at("ve")), ("se", w.at("se"))]
                     + list(extra),
                     inc=[("ve" if eng == "vector" else "se", 1)])
            w.do("sync", lambda e: e.dma_start(out=scl_sb[:, :], in_=scl),
                 inc=[("dL", 16)])
            step("vector", lambda e: e.memset(fb1[:, 3:4], 0.0))
            step("vector", lambda e: e.tensor_scalar_mul(
                fb1[:, 0:1], scl_sb[:, 0:1], 0.5), extra=[("dL", w.at("dL"))])
            step("vector", lambda e: e.tensor_tensor(
                fb1[:, 1:2], scl_sb[:, 1:2], scl_sb[:, 2:3], ALU.mult))
            step("vector", lambda e: e.tensor_scalar_mul(
                fb1[:, 1:2], fb1[:, 1:2], 0.25))
            step("vector", lambda e: e.tensor_tensor(
                fb1[:, 2:3], fb1[:, 1:2], fb1[:, 1:2], ALU.mult))
            w.do("tensor", lambda e: e.matmul(
                pbc[:, :], ones1[:, :], fb1[:, :], start=True, stop=True),
                wait=[("ve", w.at("ve")), ("gp", gp_ones)], inc=[("pe", 1)])
            step("vector", lambda e: e.tensor_copy(fb[:, :], pbc[:, :]),
                 extra=[("pe", w.at("pe"))])
            fb_ve = w.at("ve")

            emit_ln_lnT(w, x1, dict(xst=xst, lnt=lnt, sqscr=sqscr, stats=stats,
                                    ceps=ceps, lnT=lnT, ptT=ptT, ident=ident),
                        gp_c, gp_i)
            lnT_ready = w.at("ve")

            upw_free = [0, 0]
            psU_free = [0, 0]
            psG_free = [0, 0]
            psD_free_se = [0, 0]
            psD_free_ve = [0, 0]
            hT_free = 0
            acc_se = [0] * NT
            for grp in range(NGRP):
                for ec in range(GE):
                    e0 = grp * 512 + ec * 128
                    b = ec % 2
                    for half in range(2):
                        w.do("sync", lambda e, e0=e0, b=b, half=half:
                             e.dma_start(
                                 out=upw[b].rearrange(
                                     "p (i c) -> p i c", c=128)[:, 8 * half:8 * (half + 1), :],
                                 in_=tupT.rearrange(
                                     "(i p) q -> p i q", p=128)[:, 8 * half:8 * (half + 1),
                                                               e0:e0 + 128]),
                             wait=[("pe", upw_free[b])] if half == 0 else (),
                             inc=[("dW", 16)])
                        w.do("sync", lambda e, e0=e0, b=b, half=half:
                             e.dma_start(
                                 out=gw[b].rearrange(
                                     "p (i c) -> p i c", c=128)[:, 8 * half:8 * (half + 1), :],
                                 in_=tgT.rearrange(
                                     "(i p) q -> p i q", p=128)[:, 8 * half:8 * (half + 1),
                                                               e0:e0 + 128]),
                             inc=[("dW", 16)])
                    wld = w.at("dW")
                    for th in range(2):
                        pb = th
                        for i in range(16):
                            w.do("tensor", lambda e, i=i, b=b, th=th, pb=pb:
                                 e.matmul(psU[pb][:, :],
                                          upw[b][:, 128 * i:128 * (i + 1)],
                                          lnT[:, i * TLOC + 512 * th:
                                              i * TLOC + 512 * (th + 1)],
                                          start=(i == 0), stop=(i == 15)),
                                 wait=([("dW", wld), ("ve", lnT_ready),
                                        ("se", psU_free[pb])]
                                       if i == 0 else ()),
                                 inc=())
                            w.do("tensor", lambda e, i=i, b=b, th=th, pb=pb:
                                 e.matmul(psG[pb][:, :],
                                          gw[b][:, 128 * i:128 * (i + 1)],
                                          lnT[:, i * TLOC + 512 * th:
                                              i * TLOC + 512 * (th + 1)],
                                          start=(i == 0), stop=(i == 15)),
                                 wait=[("ve", psG_free[pb])] if i == 0 else (),
                                 inc=[("pe", 1)] if i == 15 else ())
                        pe_ug = w.at("pe")
                        w.do("scalar", lambda e, pb=pb: e.activation(
                            sil[:, :], psU[pb][:, :], AF.Silu,
                            scale=fb[:, 0:1]),
                            wait=[("pe", pe_ug), ("ve", fb_ve),
                                  ("se", w.at("se"))], inc=[("se", 1)])
                        psU_free[pb] = w.at("se")
                        w.do("vector", lambda e, ec=ec, th=th, pb=pb:
                             e.tensor_tensor(
                                 hT[:, ec * TLOC + 512 * th:
                                    ec * TLOC + 512 * (th + 1)],
                                 sil[:, :], psG[pb][:, :], ALU.mult),
                             wait=[("se", w.at("se")), ("pe", hT_free),
                                   ("ve", w.at("ve"))], inc=[("ve", 1)])
                        psG_free[pb] = w.at("ve")
                    upw_free[b] = w.at("pe")
                    w.do("sync", lambda e, e0=e0, ec=ec: e.dma_start(
                        out=dwg[:, ec * D:(ec + 1) * D],
                        in_=tdT[e0:e0 + 128, :]),
                        wait=[("pe", hT_free)] if ec == 0 else (),
                        inc=[("dW", 16)])
                h_ve = w.at("ve")
                dw_ld = w.at("dW")
                for tt in range(NT):
                    for dc in range(4):
                        pb = dc % 2
                        for ec in range(GE):
                            w.do("tensor", lambda e, tt=tt, dc=dc, ec=ec, pb=pb:
                                 e.matmul(psD[pb][:, :],
                                          hT[:, ec * TLOC + 128 * tt:
                                             ec * TLOC + 128 * (tt + 1)],
                                          dwg[:, ec * D + 512 * dc:
                                              ec * D + 512 * (dc + 1)],
                                          start=(ec == 0), stop=(ec == GE - 1)),
                                 wait=([("ve", max(h_ve, psD_free_ve[pb])),
                                        ("dW", dw_ld),
                                        ("se", psD_free_se[pb])]
                                       if ec == 0 else ()),
                                 inc=[("pe", 1)] if ec == GE - 1 else ())
                        if grp == 0:
                            w.do("scalar", lambda e, tt=tt, dc=dc, pb=pb:
                                 e.copy(acc[tt][:, 512 * dc:512 * (dc + 1)],
                                        psD[pb][:, :]),
                                 wait=[("pe", w.at("pe")), ("se", w.at("se"))],
                                 inc=[("se", 1)])
                            psD_free_se[pb] = w.at("se")
                        else:
                            w.do("vector", lambda e, tt=tt, dc=dc, pb=pb:
                                 e.tensor_add(
                                     acc[tt][:, 512 * dc:512 * (dc + 1)],
                                     acc[tt][:, 512 * dc:512 * (dc + 1)],
                                     psD[pb][:, :]),
                                 wait=[("pe", w.at("pe")), ("se", acc_se[tt]),
                                       ("ve", w.at("ve"))], inc=[("ve", 1)])
                            psD_free_ve[pb] = w.at("ve")
                hT_free = w.at("pe")
                for tt in range(NT):
                    acc_se[tt] = w.at("se")
            ffn_done_ve = w.at("ve")
            ffn_done_se = w.at("se")

            x_free = [0, 0]
            sqf = 0
            for tt in range(NT):
                b = tt % 2
                w.do("sync", lambda e, tt=tt, b=b: e.dma_start(
                    out=xst[b][:, :], in_=x1[tt * 128:(tt + 1) * 128, :]),
                    wait=[("ve", x_free[b]), ("se", ffn_done_se),
                          ("pe", w.at("pe"))], inc=[("dL", 16)])
                ld = w.at("dL")
                emit_stats9(w, (lambda tt=tt: acc[tt][:, :]), st2, tt, ceps,
                            gp_c, extra_first_wait=[("ve", ffn_done_ve),
                                                    ("se", ffn_done_se)],
                            sq_buf=sqscr[0], sq_free=sqf,
                            scale2=fb[:, 2:3], scalef=fb[:, 1:2])
                sqf = w.at("ve")

                def c(i, tt=tt):
                    return st2[:, tt * 9 + i:tt * 9 + i + 1]
                w.do("scalar", lambda e, tt=tt: e.activation(
                    t1[:, :], acc[tt][:, :], AF.Identity, bias=c(5),
                    scale=c(8)),
                    wait=[("ve", w.at("ve")), ("se", w.at("se"))],
                    inc=[("se", 1)])
                w.do("vector", lambda e, b=b: e.tensor_add(
                    x1t[:, :], t1[:, :], xst[b][:, :]),
                    wait=[("se", w.at("se")), ("dL", ld), ("dS", w.at("dS")),
                          ("ve", w.at("ve"))],
                    inc=[("ve", 1)])
                x_free[b] = w.at("ve")
                w.do("sync", lambda e, tt=tt: e.dma_start(
                    out=x2[tt * 128:(tt + 1) * 128, :], in_=x1t[:, :]),
                    wait=[("ve", w.at("ve"))], inc=[("dS", 16)])
            w.wait("sync", ("dS", w.at("dS")))

        build_blocked(nc, sems, body)
    return nc



def build_l1prep():
    nc = bass.Bass("TRN2", target_bir_lowering=False,
                   detect_race_conditions=False)
    x2 = nc.dram_tensor("x2", [TLOC, D], F32, kind="ExternalInput").ap()
    wdqT = nc.dram_tensor("wdqT", [D, LKV], F32, kind="ExternalInput").ap()
    wuqT = nc.dram_tensor("wuqT", [LKV, D], F32, kind="ExternalInput").ap()
    wdkvT = nc.dram_tensor("wdkvT", [D, LKV], F32, kind="ExternalInput").ap()
    wuq = nc.dram_tensor("wuq", [D, LKV], F32, kind="ExternalInput").ap()
    wdq = nc.dram_tensor("wdq", [LKV, D], F32, kind="ExternalInput").ap()
    wuk = nc.dram_tensor("wuk", [D, LKV], F32, kind="ExternalInput").ap()
    wuv = nc.dram_tensor("wuv", [D, LKV], F32, kind="ExternalInput").ap()
    woT2 = nc.dram_tensor("woT2", [D, D], F32, kind="ExternalInput").ap()
    qT2 = nc.dram_tensor("qT2", [D, TLOC], BF16, kind="ExternalOutput").ap()
    ckvT = nc.dram_tensor("ckvT", [LKV, TLOC], BF16, kind="ExternalOutput").ap()
    ckva = nc.dram_tensor("ckva", [TLOC, 513], BF16, kind="ExternalOutput").ap()
    keff = nc.dram_tensor("keff", [D, LKV], BF16, kind="ExternalOutput").ap()
    veff = nc.dram_tensor("veff", [LKV, D], BF16, kind="ExternalOutput").ap()

    with ExitStack() as ctx:
        sb = lambda n_, s, d=F32: ctx.enter_context(nc.sbuf_tensor(n_, s, d))
        ps = lambda n_, s, d=F32: ctx.enter_context(nc.psum_tensor(n_, s, d))
        sems = mk_sems(nc, ctx, SEM_NAMES)
        ident = sb("ident", [128, 128], BF16)
        ceps = sb("ceps", [128, 1])
        xst = [sb(f"xst{i}", [128, D]) for i in range(2)]
        lnt = [sb(f"lnt{i}", [128, D], BF16) for i in range(2)]
        sqscr = [sb(f"sqscr{i}", [128, D]) for i in range(2)]
        stats = sb("stats", [128, 9 * NT])
        lnT = sb("lnT", [128, 16 * TLOC], BF16)
        ptT = [ctx.enter_context(nc.psum_tensor(f"ptT{i}", [128, 128], BF16))
               for i in range(2)]
        A = sb("arena", [128, 24576], BF16)
        wdqT_b = A[:, 0:16 * LKV]
        wuqT_b = A[:, 8192:8192 + 4 * D]
        wdkvT_b = A[:, 16384:16384 + 16 * LKV]
        wuq_b = A[:, 0:16 * LKV]
        wuk_b = A[:, 8192:8192 + 16 * LKV]
        wuv_b = A[:, 16384:16384 + 16 * LKV]
        wdq_b = lnT[:, 0:4 * D]
        wstg = [sb(f"wstg{i}", [128, D]) for i in range(2)]
        wo_b = [sb(f"wo_b{i}", [128, D], BF16) for i in range(2)]
        qaT = sb("qaT", [128, 4 * TLOC], BF16)
        ckvT_sb = sb("ckvT_sb", [128, 4 * TLOC], BF16)
        m1 = sb("m1", [128, 4 * LKV], BF16)
        keff_sb = sb("keff_sb", [128, LKV], BF16)
        veff_sb = sb("veff_sb", [128, 4 * D], BF16)
        ckva_sb = [sb(f"ckva{t}", [128, 513], BF16) for t in range(NT)]
        stg = [sb(f"stg{i}", [128, TLOC], BF16) for i in range(2)]
        pq = [ps(f"pq{i}", [128, 512]) for i in range(6)]

        def cast_in(w, src, nrow_chunks, ncols, dst_view, colw, cast_track,
                    extra_wait=()):
            for i in range(nrow_chunks):
                b = i % 2
                w.do("sync", lambda e, src=src, i=i, b=b, ncols=ncols:
                     e.dma_start(out=wstg[b][:, 0:ncols],
                                 in_=src[128 * i:128 * (i + 1), :]),
                     wait=[("ve", cast_track[b])], inc=[("dW", 16)])
                w.do("vector", lambda e, i=i, b=b, ncols=ncols, colw=colw:
                     e.tensor_copy(dst_view[:, i * colw:i * colw + ncols],
                                   wstg[b][:, 0:ncols]),
                     wait=[("dW", w.at("dW"))] + (list(extra_wait)
                                                  if i == 0 else []),
                     inc=[("ve", 1)])
                cast_track[b] = w.at("ve")

        def body(w):
            gp_c, gp_i = emit_consts(w, ident, [(ceps[:, :], EPS)])
            for t in range(NT):
                w.do("gpsimd", lambda e, t=t: e.memset(
                    ckva_sb[t][:, 512:513], 1.0),
                    inc=[("gp", 1)] if t == NT - 1 else ())
            gp_ones = w.at("gp")

            emit_ln_lnT(w, x2, dict(xst=xst, lnt=lnt, sqscr=sqscr, stats=stats,
                                    ceps=ceps, lnT=lnT, ptT=ptT, ident=ident),
                        gp_c, gp_i)
            lnT_ready = w.at("ve")
            ct = [0, 0]
            cast_in(w, wdqT, 16, LKV, A, LKV, ct)
            cast_in(w, wuqT, 4, D, A[:, 8192:], D, ct)
            cast_in(w, wdkvT, 16, LKV, A[:, 16384:], LKV, ct)
            p2_ld = w.at("ve")

            pq_free = [0] * 8
            for lc in range(4):
                for th in range(2):
                    pb = (lc * 2 + th) % 6
                    for i in range(16):
                        w.do("tensor", lambda e, lc=lc, th=th, i=i, pb=pb:
                             e.matmul(pq[pb][:, :],
                                      wdqT_b[:, i * LKV + 128 * lc:
                                             i * LKV + 128 * (lc + 1)],
                                      lnT[:, i * TLOC + 512 * th:
                                          i * TLOC + 512 * (th + 1)],
                                      start=(i == 0), stop=(i == 15)),
                             wait=([("ve", max(p2_ld, lnT_ready)),
                                    ("se", pq_free[pb])] if i == 0 else ()),
                             inc=[("pe", 1)] if i == 15 else ())
                    w.do("scalar", lambda e, lc=lc, th=th, pb=pb: e.copy(
                        qaT[:, lc * TLOC + 512 * th:lc * TLOC + 512 * (th + 1)],
                        pq[pb][:, :]),
                        wait=[("pe", w.at("pe")), ("se", w.at("se"))],
                        inc=[("se", 1)])
                    pq_free[pb] = w.at("se")
            qa_se = w.at("se")
            stg_free = [0, 0]
            for cc in range(16):
                sbuf_i = cc % 2
                for th in range(2):
                    pb = (cc * 2 + th) % 6
                    for lc in range(4):
                        w.do("tensor", lambda e, cc=cc, th=th, lc=lc, pb=pb:
                             e.matmul(pq[pb][:, :],
                                      wuqT_b[:, lc * D + 128 * cc:
                                             lc * D + 128 * (cc + 1)],
                                      qaT[:, lc * TLOC + 512 * th:
                                          lc * TLOC + 512 * (th + 1)],
                                      start=(lc == 0), stop=(lc == 3)),
                             wait=([("se", max(qa_se, pq_free[pb]))]
                                   if lc == 0 else ()),
                             inc=[("pe", 1)] if lc == 3 else ())
                    w.do("scalar", lambda e, th=th, pb=pb, sbuf_i=sbuf_i:
                         e.copy(stg[sbuf_i][:, 512 * th:512 * (th + 1)],
                                pq[pb][:, :]),
                         wait=[("pe", w.at("pe")), ("dS", stg_free[sbuf_i]),
                               ("se", w.at("se"))],
                         inc=[("se", 1)])
                    pq_free[pb] = w.at("se")
                w.do("sync", lambda e, cc=cc, sbuf_i=sbuf_i: e.dma_start(
                    out=qT2[128 * cc:128 * (cc + 1), :], in_=stg[sbuf_i][:, :]),
                    wait=[("se", w.at("se"))], inc=[("dS", 16)])
                stg_free[sbuf_i] = w.at("dS")
            for lc in range(4):
                for th in range(2):
                    pb = (lc * 2 + th) % 6
                    for i in range(16):
                        w.do("tensor", lambda e, lc=lc, th=th, i=i, pb=pb:
                             e.matmul(pq[pb][:, :],
                                      wdkvT_b[:, i * LKV + 128 * lc:
                                             i * LKV + 128 * (lc + 1)],
                                      lnT[:, i * TLOC + 512 * th:
                                          i * TLOC + 512 * (th + 1)],
                                      start=(i == 0), stop=(i == 15)),
                             wait=[("se", pq_free[pb])] if i == 0 else (),
                             inc=[("pe", 1)] if i == 15 else ())
                    w.do("scalar", lambda e, lc=lc, th=th, pb=pb: e.copy(
                        ckvT_sb[:, lc * TLOC + 512 * th:
                                lc * TLOC + 512 * (th + 1)],
                        pq[pb][:, :]),
                        wait=[("pe", w.at("pe")), ("se", w.at("se"))],
                        inc=[("se", 1)])
                    pq_free[pb] = w.at("se")
            ckv_se = w.at("se")
            w.do("sync", lambda e: e.dma_start(
                out=ckvT.rearrange("(lc p) t -> p lc t", p=128),
                in_=ckvT_sb.rearrange("p (lc t) -> p lc t", t=TLOC)),
                wait=[("se", ckv_se)], inc=[("dS", 16)])
            for lc in range(4):
                for tt in range(NT):
                    pb = tt % 2
                    w.do("tensor", lambda e, lc=lc, tt=tt, pb=pb: e.transpose(
                        ptT[pb][:, :],
                        ckvT_sb[:, lc * TLOC + 128 * tt:
                                lc * TLOC + 128 * (tt + 1)], ident[:, :]),
                        wait=[("se", ckv_se), ("ve", w.at("ve")), ("gp", gp_i)],
                        inc=[("pe", 1)])
                    w.do("vector", lambda e, lc=lc, tt=tt, pb=pb: e.tensor_copy(
                        ckva_sb[tt][:, 128 * lc:128 * (lc + 1)], ptT[pb][:, :]),
                        wait=[("pe", w.at("pe"))], inc=[("ve", 1)])
            for tt in range(NT):
                w.do("sync", lambda e, tt=tt: e.dma_start(
                    out=ckva[128 * tt:128 * (tt + 1), :], in_=ckva_sb[tt][:, :]),
                    wait=[("ve", w.at("ve")), ("gp", gp_ones)], inc=[("dS", 16)])

            ct3 = [ct[0], ct[1]]
            cast_in(w, wuq, 16, LKV, A, LKV, ct3,
                    extra_wait=[("pe", w.at("pe")), ("se", w.at("se"))])
            cast_in(w, wuk, 16, LKV, A[:, 8192:], LKV, ct3)
            cast_in(w, wdq, 4, D, lnT, D, ct3,
                    extra_wait=[("pe", w.at("pe"))])
            p3_ld = w.at("ve")
            for lc in range(4):
                pb = lc % 6
                for i in range(16):
                    w.do("tensor", lambda e, lc=lc, i=i, pb=pb: e.matmul(
                        pq[pb][:, :],
                        wuq_b[:, i * LKV + 128 * lc:i * LKV + 128 * (lc + 1)],
                        wuk_b[:, i * LKV:(i + 1) * LKV],
                        start=(i == 0), stop=(i == 15)),
                        wait=([("ve", p3_ld), ("se", pq_free[pb])]
                              if i == 0 else ()),
                        inc=[("pe", 1)] if i == 15 else ())
                w.do("scalar", lambda e, lc=lc, pb=pb: e.copy(
                    m1[:, lc * LKV:(lc + 1) * LKV], pq[pb][:, :]),
                    wait=[("pe", w.at("pe")), ("se", w.at("se"))],
                    inc=[("se", 1)])
                pq_free[pb] = w.at("se")
            m1_se = w.at("se")
            for cc in range(16):
                pb = cc % 6
                for ac in range(4):
                    w.do("tensor", lambda e, cc=cc, ac=ac, pb=pb: e.matmul(
                        pq[pb][:, :],
                        wdq_b[:, ac * D + 128 * cc:ac * D + 128 * (cc + 1)],
                        m1[:, ac * LKV:(ac + 1) * LKV],
                        start=(ac == 0), stop=(ac == 3)),
                        wait=([("se", max(m1_se, pq_free[pb]))]
                              if ac == 0 else ()),
                        inc=[("pe", 1)] if ac == 3 else ())
                w.do("scalar", lambda e, cc=cc, pb=pb: e.copy(
                    keff_sb[:, :], pq[pb][:, :]),
                    wait=[("pe", w.at("pe")), ("dS", w.at("dS")),
                          ("se", w.at("se"))],
                    inc=[("se", 1)])
                pq_free[pb] = w.at("se")
                w.do("sync", lambda e, cc=cc: e.dma_start(
                    out=keff[128 * cc:128 * (cc + 1), :], in_=keff_sb[:, :]),
                    wait=[("se", w.at("se"))], inc=[("dS", 16)])

            ct4 = [ct3[0], ct3[1]]
            cast_in(w, wuv, 16, LKV, A[:, 16384:], LKV, ct4,
                    extra_wait=[("pe", w.at("pe"))])
            wuv_ld = w.at("ve")
            wo_cast = [0, 0]
            for chp in range(4):
                for i in range(16):
                    b = i % 2
                    w.do("sync", lambda e, i=i, b=b, chp=chp: e.dma_start(
                        out=wstg[b][:, 0:512],
                        in_=woT2[128 * i:128 * (i + 1),
                                 512 * chp:512 * (chp + 1)]),
                        wait=[("ve", ct4[b])], inc=[("dW", 16)])
                    w.do("vector", lambda e, i=i, b=b: e.tensor_copy(
                        wo_b[b][:, 0:512], wstg[b][:, 0:512]),
                        wait=[("dW", w.at("dW")), ("pe", wo_cast[b])],
                        inc=[("ve", 1)])
                    ct4[b] = w.at("ve")
                    wo_ld = w.at("ve")
                    for lc in range(4):
                        w.do("tensor", lambda e, lc=lc, i=i, b=b: e.matmul(
                            pq[lc][:, :],
                            wuv_b[:, i * LKV + 128 * lc:
                                   i * LKV + 128 * (lc + 1)],
                            wo_b[b][:, 0:512],
                            start=(i == 0), stop=(i == 15)),
                            wait=([("ve", max(wuv_ld, wo_ld)),
                                   ("se", pq_free[lc])] if i == 0 else
                                  ([("ve", wo_ld)] if lc == 0 else ())),
                            inc=[("pe", 1)] if lc == 3 else ())
                    wo_cast[b] = w.at("pe")
                for lc in range(4):
                    w.do("scalar", lambda e, lc=lc, chp=chp: e.copy(
                        veff_sb[:, lc * D + 512 * chp:lc * D + 512 * (chp + 1)],
                        pq[lc][:, :]),
                        wait=[("pe", w.at("pe")), ("se", w.at("se"))],
                        inc=[("se", 1)])
                    pq_free[lc] = w.at("se")
            w.do("sync", lambda e: e.dma_start(
                out=veff.rearrange("(lc p) t -> p lc t", p=128),
                in_=veff_sb.rearrange("p (lc t) -> p lc t", t=D)),
                wait=[("se", w.at("se"))], inc=[("dS", 16)])
            w.wait("sync", ("dS", w.at("dS")))

        build_blocked(nc, sems, body)
    return nc



def build_l1mat():
    nc = bass.Bass("TRN2", target_bir_lowering=False,
                   detect_race_conditions=False)
    lut = nc.dram_tensor("lut", [1, 16], F32, kind="ExternalInput").ap()
    gidxT_sh = nc.dram_tensor("gidxT_sh", [256, INTER], I32,
                              kind="ExternalInput").ap()
    uidxT_sh = nc.dram_tensor("uidxT_sh", [256, INTER], I32,
                              kind="ExternalInput").ap()
    didxT_sh = nc.dram_tensor("didxT_sh", [768, D], I32,
                              kind="ExternalInput").ap()
    wgT_sh = nc.dram_tensor("wgT_sh", [256, INTER], BF16,
                            kind="ExternalOutput").ap()
    wuT_sh = nc.dram_tensor("wuT_sh", [256, INTER], BF16,
                            kind="ExternalOutput").ap()
    wdT2_sh = nc.dram_tensor("wdT2_sh", [768, D], BF16,
                             kind="ExternalOutput").ap()

    with ExitStack() as ctx:
        sb = lambda n_, s, d=F32: ctx.enter_context(nc.sbuf_tensor(n_, s, d))
        ps = lambda n_, s, d=F32: ctx.enter_context(nc.psum_tensor(n_, s, d))
        sems = mk_sems(nc, ctx, SEM_NAMES)
        ones1 = sb("ones1", [1, 128])
        lut1 = sb("lut1", [1, 16])
        lutb = sb("lutb", [128, 16])
        cks = sb("cks", [128, 15])
        sumc = sb("sumc", [128, 1])
        a0 = sb("a0", [128, 1])
        bk = sb("bk", [128, 15])
        pbc = ps("pbc", [128, 16])
        ist = [sb(f"ist{i}", [128, INTER], I32) for i in range(2)]
        idxf = sb("idxf", [128, INTER])
        sg = sb("sg", [128, INTER])
        acc = [sb(f"macc{i}", [128, INTER]) for i in range(2)]
        tout = [sb(f"tout{i}", [128, INTER], BF16) for i in range(2)]

        def body(w):
            def step(eng, fn, extra=()):
                w.do(eng, fn, wait=[("ve", w.at("ve")), ("se", w.at("se"))]
                     + list(extra),
                     inc=[("ve" if eng == "vector" else "se", 1)])
            w.do("gpsimd", lambda e: e.memset(ones1[:, :], 1.0), inc=[("gp", 1)])
            for k in range(15):
                w.do("gpsimd", lambda e, k=k: e.memset(
                    bk[:, k:k + 1], -(k + 0.5)),
                    inc=[("gp", 1)] if k == 14 else ())
            gp_ok = w.at("gp")
            w.do("sync", lambda e: e.dma_start(out=lut1[:, :], in_=lut),
                 inc=[("dL", 16)])
            w.do("tensor", lambda e: e.matmul(
                pbc[:, :], ones1[:, :], lut1[:, :], start=True, stop=True),
                wait=[("dL", w.at("dL")), ("gp", gp_ok)], inc=[("pe", 1)])
            step("vector", lambda e: e.tensor_copy(lutb[:, :], pbc[:, :]),
                 extra=[("pe", w.at("pe"))])
            step("vector", lambda e: e.tensor_sub(
                cks[:, :], lutb[:, 1:16], lutb[:, 0:15]))
            step("vector", lambda e: e.tensor_scalar_mul(
                cks[:, :], cks[:, :], 0.5))
            step("vector", lambda e: e.reduce_sum(
                sumc[:, :], cks[:, :], axis=AX.X))
            step("vector", lambda e: e.tensor_add(
                a0[:, :], lutb[:, 0:1], sumc[:, :]))
            coef_ve = w.at("ve")

            tiles = ([(gidxT_sh, wgT_sh, r, INTER) for r in range(2)]
                     + [(uidxT_sh, wuT_sh, r, INTER) for r in range(2)]
                     + [(didxT_sh, wdT2_sh, r, D) for r in range(6)])
            ist_free = [0, 0]
            tout_free = [0, 0]
            for k, (src, dst, r, F) in enumerate(tiles):
                b = k % 2
                w.do("sync", lambda e, src=src, r=r, b=b, F=F: e.dma_start(
                    out=ist[b][:, 0:F], in_=src[128 * r:128 * (r + 1), :]),
                    wait=[("ve", ist_free[b])], inc=[("dL", 16)])
                step("vector", lambda e, b=b, F=F: e.tensor_copy(
                    idxf[:, 0:F], ist[b][:, 0:F]),
                    extra=[("dL", w.at("dL")), ("ve", coef_ve)])
                ist_free[b] = w.at("ve")
                step("scalar", lambda e, F=F: e.activation(
                    sg[:, 0:F], idxf[:, 0:F], AF.Sign, bias=bk[:, 0:1]))
                step("scalar", lambda e, b=b, F=F: e.activation(
                    acc[0][:, 0:F], sg[:, 0:F], AF.Identity,
                    scale=cks[:, 0:1], bias=a0[:, :]))
                cur = 0
                for kk in range(1, 15):
                    step("scalar", lambda e, kk=kk, F=F: e.activation(
                        sg[:, 0:F], idxf[:, 0:F], AF.Sign, bias=bk[:, kk:kk + 1]))
                    step("vector", lambda e, kk=kk, F=F, cur=cur:
                         e.scalar_tensor_tensor(
                             acc[1 - cur][:, 0:F], sg[:, 0:F], cks[:, kk:kk + 1],
                             acc[cur][:, 0:F], ALU.mult, ALU.add))
                    cur = 1 - cur
                w.do("vector", lambda e, b=b, F=F, cur=cur: e.tensor_copy(
                    tout[b][:, 0:F], acc[cur][:, 0:F]),
                    wait=[("ve", w.at("ve")), ("se", w.at("se")),
                          ("dS", tout_free[b])],
                    inc=[("ve", 1)])
                w.do("sync", lambda e, dst=dst, r=r, b=b, F=F: e.dma_start(
                    out=dst[128 * r:128 * (r + 1), :], in_=tout[b][:, 0:F]),
                    wait=[("ve", w.at("ve"))], inc=[("dS", 16)])
                tout_free[b] = w.at("dS")
            w.wait("sync", ("dS", w.at("dS")))

        build_blocked(nc, sems, body)
    return nc



def build_l1attn():
    nc = bass.Bass("TRN2", target_bir_lowering=False,
                   detect_race_conditions=False)
    qT2h = nc.dram_tensor("qT2h", [256, S], BF16, kind="ExternalInput").ap()
    ckvTf = nc.dram_tensor("ckvTf", [LKV, S], BF16, kind="ExternalInput").ap()
    ckvaf = nc.dram_tensor("ckvaf", [S, 513], BF16, kind="ExternalInput").ap()
    keffh = nc.dram_tensor("keffh", [256, LKV], BF16, kind="ExternalInput").ap()
    veffh = nc.dram_tensor("veffh", [LKV, 256], BF16, kind="ExternalInput").ap()
    yh = nc.dram_tensor("yh", [S, 256], BF16, kind="ExternalOutput").ap()

    with ExitStack() as ctx:
        sb = lambda n_, s, d=F32: ctx.enter_context(nc.sbuf_tensor(n_, s, d))
        ps = lambda n_, s, d=F32: ctx.enter_context(nc.psum_tensor(n_, s, d))
        sems = mk_sems(nc, ctx, SEM_NAMES)
        cshift = sb("cshift", [128, 1])
        onesF = sb("onesF", [1, 1])
        ckvT_sb = sb("ckvT_sb", [128, 4 * S], BF16)
        ckva_sb = sb("ckva_sb", [128, 64 * 513], BF16)
        qh_sb = sb("qh_sb", [128, S], BF16)
        keff_sb = sb("keff_sb", [128, 2 * LKV], BF16)
        veff_sb = sb("veff_sb", [128, 4 * 256], BF16)
        qlatT = sb("qlatT", [128, 4 * 512], BF16)
        ptb = [sb(f"ptb{i}", [128, 512], BF16) for i in range(2)]
        ctxT = sb("ctxT", [128, 4 * 512], BF16)
        sums = sb("sums", [1, 512])
        rq = sb("rq", [128, 4])
        y_sb = sb("y_sb", [128, 64 * 256], BF16)
        psS = [ps(f"psS{i}", [128, 512]) for i in range(2)]
        psC = [ps(f"psC{i}", [128, 512]) for i in range(4)]
        psSm = ps("psSm", [1, 512])
        psY = ps("psY", [128, 128])

        def body(w):
            w.do("gpsimd", lambda e: e.memset(cshift[:, :], -SHIFT1),
                 inc=[("gp", 1)])
            w.do("gpsimd", lambda e: e.memset(onesF[:, :], 1.0), inc=[("gp", 1)])
            gp_c = w.at("gp")
            for g in range(4):
                w.do("sync", lambda e, g=g: e.dma_start(
                    out=ckvT_sb.rearrange("p (lc s) -> p lc s", lc=4)[:, g:g + 1, :],
                    in_=ckvTf.rearrange("(lc p) s -> p lc s", p=128)[:, g:g + 1, :]),
                    inc=[("dL", 16)])
            for g in range(4):
                w.do("sync", lambda e, g=g: e.dma_start(
                    out=ckva_sb.rearrange("p (st c) -> p st c", c=513)[:, 16 * g:16 * (g + 1), :],
                    in_=ckvaf.rearrange("(st p) c -> p st c", p=128)[:, 16 * g:16 * (g + 1), :]),
                    inc=[("dL", 16)])
            w.do("sync", lambda e: e.dma_start(
                out=keff_sb.rearrange("p (hh l) -> p hh l", hh=2),
                in_=keffh.rearrange("(hh p) l -> p hh l", p=128)),
                inc=[("dL", 16)])
            w.do("sync", lambda e: e.dma_start(
                out=veff_sb.rearrange("p (lc c) -> p lc c", lc=4),
                in_=veffh.rearrange("(lc p) c -> p lc c", p=128)),
                inc=[("dL", 16)])
            all_ld = w.at("dL")

            psS_free = [0, 0]
            ptb_free = [0, 0]
            psC_free = 0
            psSm_free = 0
            psY_free = 0
            for hh in range(2):
                w.do("sync", lambda e, hh=hh: e.dma_start(
                    out=qh_sb[:, :],
                    in_=qT2h[128 * hh:128 * (hh + 1), :]),
                    wait=[("pe", w.at("pe"))], inc=[("dL", 16)])
                all_ld = w.at("dL")
                for qb in range(16):
                    for lc in range(4):
                        pb = lc % 2
                        w.do("tensor", lambda e, hh=hh, qb=qb, lc=lc, pb=pb:
                             e.matmul(psS[pb][:, :],
                                      keff_sb[:, hh * LKV + 128 * lc:
                                              hh * LKV + 128 * (lc + 1)],
                                      qh_sb[:, 512 * qb:512 * (qb + 1)],
                                      start=True, stop=True),
                             wait=[("dL", all_ld), ("se", psS_free[pb])],
                             inc=[("pe", 1)])
                        w.do("scalar", lambda e, lc=lc, pb=pb: e.copy(
                            qlatT[:, 512 * lc:512 * (lc + 1)], psS[pb][:, :]),
                            wait=[("pe", w.at("pe")), ("se", w.at("se"))],
                            inc=[("se", 1)])
                        psS_free[pb] = w.at("se")
                    qlat_se = w.at("se")
                    nst = 4 * qb + 4
                    for stt in range(nst):
                        sb_i = stt % 2
                        for lc in range(4):
                            w.do("tensor", lambda e, stt=stt, lc=lc, sb_i=sb_i:
                                 e.matmul(psS[sb_i][:, :],
                                          ckvT_sb[:, lc * S + 128 * stt:
                                                  lc * S + 128 * (stt + 1)],
                                          qlatT[:, 512 * lc:512 * (lc + 1)],
                                          start=(lc == 0), stop=(lc == 3)),
                                 wait=([("se", max(qlat_se, psS_free[sb_i]))]
                                       if lc == 0 else ()),
                                 inc=[("pe", 1)] if lc == 3 else ())
                        pe_s = w.at("pe")
                        w.do("scalar", lambda e, sb_i=sb_i: e.activation(
                            ptb[sb_i][:, :], psS[sb_i][:, :], AF.Exp,
                            scale=QKS, bias=cshift[:, :]),
                            wait=[("pe", pe_s), ("gp", gp_c),
                                  ("pe", ptb_free[sb_i])],
                            inc=[("se", 1)])
                        psS_free[sb_i] = w.at("se")
                        se_exp = w.at("se")
                        if stt >= 4 * qb:
                            w.do("gpsimd", lambda e, qb=qb, stt=stt, sb_i=sb_i:
                                 e.affine_select(
                                     out=ptb[sb_i][:, :], in_=ptb[sb_i][:, :],
                                     compare_op=ALU.is_ge, fill=0.0,
                                     base=512 * qb - 128 * stt,
                                     pattern=[[1, 512]], channel_multiplier=-1),
                                 wait=[("se", se_exp), ("gp", w.at("gp"))],
                                 inc=[("gp", 1)])
                        gp_mask = w.at("gp")
                        for lc in range(4):
                            w.do("tensor", lambda e, stt=stt, lc=lc, sb_i=sb_i,
                                 nst=nst: e.matmul(
                                     psC[lc][:, :],
                                     ckva_sb[:, 513 * stt + 128 * lc:
                                             513 * stt + 128 * (lc + 1)],
                                     ptb[sb_i][:, :],
                                     start=(stt == 0), stop=(stt == nst - 1)),
                                 wait=([("se", se_exp), ("gp", gp_mask),
                                        ("se", psC_free)] if lc == 0 else ()),
                                 inc=())
                        w.do("tensor", lambda e, stt=stt, sb_i=sb_i, nst=nst:
                             e.matmul(psSm[:, :],
                                      ckva_sb[:, 513 * stt + 512:513 * stt + 513],
                                      ptb[sb_i][:, :],
                                      start=(stt == 0), stop=(stt == nst - 1)),
                             wait=[("se", psSm_free)] if stt == 0 else (),
                             inc=[("pe", 1)])
                        ptb_free[sb_i] = w.at("pe")
                    pe_av = w.at("pe")
                    for lc in range(4):
                        w.do("scalar", lambda e, lc=lc: e.copy(
                            ctxT[:, 512 * lc:512 * (lc + 1)], psC[lc][:, :]),
                            wait=[("pe", pe_av), ("se", w.at("se"))],
                            inc=[("se", 1)])
                    psC_free = w.at("se")
                    w.do("scalar", lambda e: e.copy(sums[:, :], psSm[:, :]),
                         wait=[("se", w.at("se"))], inc=[("se", 1)])
                    psSm_free = w.at("se")
                    ctx_se = w.at("se")
                    for j in range(4):
                        w.do("tensor", lambda e, j=j: e.matmul(
                            psY[:, 0:1], sums[0:1, 128 * j:128 * (j + 1)],
                            onesF[:, :], start=True, stop=True),
                            wait=[("se", max(ctx_se, psY_free)),
                                  ("gp", gp_c)], inc=[("pe", 1)])
                        w.do("vector", lambda e, j=j: e.reciprocal(
                            rq[:, j:j + 1], psY[:, 0:1]),
                            wait=[("pe", w.at("pe")), ("ve", w.at("ve"))],
                            inc=[("ve", 1)])
                        rq_ve = w.at("ve")
                        for lc in range(4):
                            w.do("tensor", lambda e, j=j, lc=lc, hh=hh:
                                 e.matmul(psY[:, :],
                                          ctxT[:, 512 * lc + 128 * j:
                                               512 * lc + 128 * (j + 1)],
                                          veff_sb[:, 256 * lc + 128 * hh:
                                                  256 * lc + 128 * (hh + 1)],
                                          start=(lc == 0), stop=(lc == 3)),
                                 wait=[("ve", rq_ve)] if lc == 0 else (),
                                 inc=[("pe", 1)] if lc == 3 else ())
                        w.do("scalar", lambda e, j=j, qb=qb, hh=hh: e.activation(
                            y_sb[:, (4 * qb + j) * 256 + 128 * hh:
                                 (4 * qb + j) * 256 + 128 * (hh + 1)],
                            psY[:, :], AF.Identity, scale=rq[:, j:j + 1]),
                            wait=[("pe", w.at("pe")), ("se", w.at("se"))],
                            inc=[("se", 1)])
                        psY_free = w.at("se")
            y_se = w.at("se")
            for g in range(4):
                w.do("sync", lambda e, g=g: e.dma_start(
                    out=yh.rearrange("(g p) c -> p g c", p=128)[:, 16 * g:16 * (g + 1), :],
                    in_=y_sb.rearrange("p (g c) -> p g c", c=256)[:, 16 * g:16 * (g + 1), :]),
                    wait=[("se", y_se)], inc=[("dS", 16)])
            w.wait("sync", ("dS", w.at("dS")))

        build_blocked(nc, sems, body)
    return nc



def build_l1postffn():
    nc = bass.Bass("TRN2", target_bir_lowering=False,
                   detect_race_conditions=False)
    x2 = nc.dram_tensor("x2", [TLOC, D], F32, kind="ExternalInput").ap()
    yl = nc.dram_tensor("yl", [TLOC, D], BF16, kind="ExternalInput").ap()
    wgT2 = nc.dram_tensor("wgT2", [D, INTER], BF16, kind="ExternalInput").ap()
    wuT2 = nc.dram_tensor("wuT2", [D, INTER], BF16, kind="ExternalInput").ap()
    wdT2 = nc.dram_tensor("wdT2", [INTER, D], BF16, kind="ExternalInput").ap()
    x4 = nc.dram_tensor("x4", [TLOC, D], F32, kind="ExternalOutput").ap()
    scr_x3 = nc.dram_tensor("scr_x3", [TLOC, D], F32).ap()

    GRPS = [4] * 10 + [3]
    with ExitStack() as ctx:
        sb = lambda n_, s, d=F32: ctx.enter_context(nc.sbuf_tensor(n_, s, d))
        ps = lambda n_, s, d=F32: ctx.enter_context(nc.psum_tensor(n_, s, d))
        sems = mk_sems(nc, ctx, SEM_NAMES)
        ident = sb("ident", [128, 128], BF16)
        ceps = sb("ceps", [128, 1])
        xst = [sb(f"xst{i}", [128, D]) for i in range(2)]
        ylt = [sb(f"ylt{i}", [128, D], BF16) for i in range(2)]
        lnt = [sb(f"lnt{i}", [128, D], BF16) for i in range(2)]
        sqscr = [sb(f"sqscr{i}", [128, D]) for i in range(2)]
        stats = sb("stats", [128, 9 * NT])
        lnT = sb("lnT", [128, 16 * TLOC], BF16)
        ptT = [ctx.enter_context(nc.psum_tensor(f"ptT{i}", [128, 128], BF16))
               for i in range(2)]
        upw = [sb(f"upw{i}", [128, 16 * 128], BF16) for i in range(2)]
        gw = [sb(f"gw{i}", [128, 16 * 128], BF16) for i in range(2)]
        sil = sb("sil", [128, 512])
        hT = sb("hT", [128, 4 * TLOC], BF16)
        dwg = sb("dwg", [128, 4 * D], BF16)
        acc = [sb(f"acc{t}", [128, D]) for t in range(NT)]
        psU = [ps(f"psU{i}", [128, 512]) for i in range(2)]
        psG = [ps(f"psG{i}", [128, 512]) for i in range(2)]
        psD = [ps(f"psD{i}", [128, 512]) for i in range(2)]
        t1 = sb("t1", [128, D])
        x1t = sb("x1t", [128, D])
        st2 = sb("st2", [128, 9 * NT])

        def body(w):
            gp_c, gp_i = emit_consts(w, ident, [(ceps[:, :], EPS)])

            x_free = [0, 0]
            y_free = [0, 0]
            sqf = 0
            for tt in range(NT):
                b = tt % 2
                w.do("sync", lambda e, tt=tt, b=b: e.dma_start(
                    out=xst[b][:, :], in_=x2[tt * 128:(tt + 1) * 128, :]),
                    wait=[("ve", x_free[b])], inc=[("dL", 16)])
                w.do("sync", lambda e, tt=tt, b=b: e.dma_start(
                    out=ylt[b][:, :], in_=yl[tt * 128:(tt + 1) * 128, :]),
                    wait=[("ve", y_free[b])], inc=[("dL", 16)])
                ld = w.at("dL")
                emit_stats9(w, (lambda b=b: ylt[b][:, :]), st2, tt, ceps,
                            gp_c, extra_first_wait=[("dL", ld)],
                            sq_buf=sqscr[0], sq_free=sqf)
                sqf = w.at("ve")

                def c(i, tt=tt):
                    return st2[:, tt * 9 + i:tt * 9 + i + 1]
                w.do("scalar", lambda e, tt=tt, b=b: e.activation(
                    t1[:, :], ylt[b][:, :], AF.Identity, bias=c(5),
                    scale=c(8)),
                    wait=[("ve", w.at("ve")), ("se", w.at("se"))],
                    inc=[("se", 1)])
                y_free[b] = w.at("ve") + 1
                w.do("vector", lambda e, b=b: e.tensor_add(
                    x1t[:, :], t1[:, :], xst[b][:, :]),
                    wait=[("se", w.at("se")), ("dS", w.at("dS")),
                          ("ve", w.at("ve"))],
                    inc=[("ve", 1)])
                x_free[b] = w.at("ve")
                w.do("sync", lambda e, tt=tt: e.dma_start(
                    out=scr_x3[tt * 128:(tt + 1) * 128, :], in_=x1t[:, :]),
                    wait=[("ve", w.at("ve"))], inc=[("dS", 16)])
            x3_st = w.at("dS")
            w.wait("sync", ("dS", x3_st))

            emit_ln_lnT(w, scr_x3, dict(xst=xst, lnt=lnt, sqscr=sqscr,
                                        stats=stats, ceps=ceps, lnT=lnT,
                                        ptT=ptT, ident=ident),
                        gp_c, gp_i, x_wait_extra=[("dS", x3_st)])
            lnT_ready = w.at("ve")

            upw_free = [0, 0]
            psU_free = [0, 0]
            psG_free = [0, 0]
            psD_free_se = [0, 0]
            psD_free_ve = [0, 0]
            hT_free = 0
            acc_se = [0] * NT
            e_base = 0
            for grp, GE in enumerate(GRPS):
                for ec in range(GE):
                    e0 = (e_base + ec) * 128
                    b = ec % 2
                    for half in range(2):
                        w.do("sync", lambda e, e0=e0, b=b, half=half:
                             e.dma_start(
                                 out=upw[b].rearrange(
                                     "p (i c) -> p i c", c=128)[:, 8 * half:8 * (half + 1), :],
                                 in_=wuT2.rearrange(
                                     "(i p) q -> p i q", p=128)[:, 8 * half:8 * (half + 1),
                                                               e0:e0 + 128]),
                             wait=[("pe", upw_free[b])] if half == 0 else (),
                             inc=[("dW", 16)])
                        w.do("sync", lambda e, e0=e0, b=b, half=half:
                             e.dma_start(
                                 out=gw[b].rearrange(
                                     "p (i c) -> p i c", c=128)[:, 8 * half:8 * (half + 1), :],
                                 in_=wgT2.rearrange(
                                     "(i p) q -> p i q", p=128)[:, 8 * half:8 * (half + 1),
                                                               e0:e0 + 128]),
                             inc=[("dW", 16)])
                    wld = w.at("dW")
                    for th in range(2):
                        pb = th
                        for i in range(16):
                            w.do("tensor", lambda e, i=i, b=b, th=th, pb=pb:
                                 e.matmul(psU[pb][:, :],
                                          upw[b][:, 128 * i:128 * (i + 1)],
                                          lnT[:, i * TLOC + 512 * th:
                                              i * TLOC + 512 * (th + 1)],
                                          start=(i == 0), stop=(i == 15)),
                                 wait=([("dW", wld),
                                        ("ve", max(lnT_ready,
                                                   psU_free[pb]))]
                                       if i == 0 else ()),
                                 inc=())
                            w.do("tensor", lambda e, i=i, b=b, th=th, pb=pb:
                                 e.matmul(psG[pb][:, :],
                                          gw[b][:, 128 * i:128 * (i + 1)],
                                          lnT[:, i * TLOC + 512 * th:
                                              i * TLOC + 512 * (th + 1)],
                                          start=(i == 0), stop=(i == 15)),
                                 wait=[("se", psG_free[pb])] if i == 0 else (),
                                 inc=[("pe", 1)] if i == 15 else ())
                        pe_ug = w.at("pe")
                        w.do("scalar", lambda e, pb=pb: e.activation(
                            sil[:, :], psG[pb][:, :], AF.Silu),
                            wait=[("pe", pe_ug), ("se", w.at("se"))],
                            inc=[("se", 1)])
                        psG_free[pb] = w.at("se")
                        w.do("vector", lambda e, ec=ec, th=th, pb=pb:
                             e.tensor_tensor(
                                 hT[:, ec * TLOC + 512 * th:
                                    ec * TLOC + 512 * (th + 1)],
                                 sil[:, :], psU[pb][:, :], ALU.mult),
                             wait=[("se", w.at("se")), ("pe", hT_free),
                                   ("ve", w.at("ve"))], inc=[("ve", 1)])
                        psU_free[pb] = w.at("ve")
                        if th == 0:
                            w.do("sync", lambda e, e0=e0, ec=ec: e.dma_start(
                                out=dwg[:, ec * D:(ec + 1) * D],
                                in_=wdT2[e0:e0 + 128, :]),
                                wait=([("pe", hT_free)] if ec == 0 else ()),
                                inc=[("dW", 16)])
                    upw_free[b] = w.at("pe")
                h_ve = w.at("ve")
                dw_ld = w.at("dW")
                for tt in range(NT):
                    for dc in range(4):
                        pb = dc % 2
                        for ec in range(GE):
                            w.do("tensor", lambda e, tt=tt, dc=dc, ec=ec, pb=pb,
                                 GE=GE: e.matmul(
                                     psD[pb][:, :],
                                     hT[:, ec * TLOC + 128 * tt:
                                        ec * TLOC + 128 * (tt + 1)],
                                     dwg[:, ec * D + 512 * dc:
                                         ec * D + 512 * (dc + 1)],
                                     start=(ec == 0), stop=(ec == GE - 1)),
                                 wait=([("ve", max(h_ve, psD_free_ve[pb])),
                                        ("dW", dw_ld),
                                        ("se", psD_free_se[pb])]
                                       if ec == 0 else ()),
                                 inc=[("pe", 1)] if ec == GE - 1 else ())
                        if grp == 0:
                            w.do("scalar", lambda e, tt=tt, dc=dc, pb=pb:
                                 e.copy(acc[tt][:, 512 * dc:512 * (dc + 1)],
                                        psD[pb][:, :]),
                                 wait=[("pe", w.at("pe")), ("se", w.at("se"))],
                                 inc=[("se", 1)])
                            psD_free_se[pb] = w.at("se")
                        else:
                            w.do("vector", lambda e, tt=tt, dc=dc, pb=pb:
                                 e.tensor_add(
                                     acc[tt][:, 512 * dc:512 * (dc + 1)],
                                     acc[tt][:, 512 * dc:512 * (dc + 1)],
                                     psD[pb][:, :]),
                                 wait=[("pe", w.at("pe")), ("se", acc_se[tt]),
                                       ("ve", w.at("ve"))], inc=[("ve", 1)])
                            psD_free_ve[pb] = w.at("ve")
                hT_free = w.at("pe")
                for tt in range(NT):
                    acc_se[tt] = w.at("se")
                e_base += GE
            ffn_done_ve = w.at("ve")
            ffn_done_se = w.at("se")

            x_free = [0, 0]
            sqf = 0
            for tt in range(NT):
                b = tt % 2
                w.do("sync", lambda e, tt=tt, b=b: e.dma_start(
                    out=xst[b][:, :], in_=scr_x3[tt * 128:(tt + 1) * 128, :]),
                    wait=[("ve", x_free[b]), ("se", ffn_done_se),
                          ("pe", w.at("pe"))], inc=[("dL", 16)])
                ld = w.at("dL")
                emit_stats9(w, (lambda tt=tt: acc[tt][:, :]), st2, tt, ceps,
                            gp_c, extra_first_wait=[("ve", ffn_done_ve),
                                                    ("se", ffn_done_se)],
                            sq_buf=sqscr[0], sq_free=sqf)
                sqf = w.at("ve")

                def c(i, tt=tt):
                    return st2[:, tt * 9 + i:tt * 9 + i + 1]
                w.do("scalar", lambda e, tt=tt: e.activation(
                    t1[:, :], acc[tt][:, :], AF.Identity, bias=c(5),
                    scale=c(8)),
                    wait=[("ve", w.at("ve")), ("se", w.at("se"))],
                    inc=[("se", 1)])
                w.do("vector", lambda e, b=b: e.tensor_add(
                    x1t[:, :], t1[:, :], xst[b][:, :]),
                    wait=[("se", w.at("se")), ("dL", ld), ("dS", w.at("dS")),
                          ("ve", w.at("ve"))],
                    inc=[("ve", 1)])
                x_free[b] = w.at("ve")
                w.do("sync", lambda e, tt=tt: e.dma_start(
                    out=x4[tt * 128:(tt + 1) * 128, :], in_=x1t[:, :]),
                    wait=[("ve", w.at("ve"))], inc=[("dS", 16)])
            w.wait("sync", ("dS", w.at("dS")))

        build_blocked(nc, sems, body)
    return nc


_BUILD_CACHE = {}


def _get(name, builder):
    if name not in _BUILD_CACHE:
        _BUILD_CACHE[name] = builder()
    return _BUILD_CACHE[name]


def _run(nc, in_maps):
    res = run_bass_kernel_spmd(nc, in_maps, core_ids=list(range(NCORES)))
    return res.results


def kernel(L0_attn_x, L0_attn_w_qkv, L0_attn_w_out,
           L0_ffn_W_up, L0_ffn_W_gate, L0_ffn_W_down,
           L1_attn_Wdq, L1_attn_Wuq, L1_attn_Wdkv, L1_attn_Wuk,
           L1_attn_Wuv, L1_attn_Wo,
           L1_ffn_lut, L1_ffn_gate_idx, L1_ffn_up_idx, L1_ffn_down_idx):
    import ml_dtypes
    ct = np.ascontiguousarray
    x = ct(np.asarray(L0_attn_x, np.float32)[0])
    wqkvT = ct(np.asarray(L0_attn_w_qkv, np.float32).T)
    woT = ct(np.asarray(L0_attn_w_out, np.float32).T)
    wupT = ct(np.asarray(L0_ffn_W_up, np.float32).T)
    wgT = ct(np.asarray(L0_ffn_W_gate, np.float32).T)
    wdT = ct(np.asarray(L0_ffn_W_down, np.float32).T)
    Wdq = np.asarray(L1_attn_Wdq, np.float32)
    Wuq = np.asarray(L1_attn_Wuq, np.float32)
    Wdkv = np.asarray(L1_attn_Wdkv, np.float32)
    Wuk = ct(np.asarray(L1_attn_Wuk, np.float32))
    Wuv = ct(np.asarray(L1_attn_Wuv, np.float32))
    Wo = np.asarray(L1_attn_Wo, np.float32)
    lut = ct(np.asarray(L1_ffn_lut, np.float32).reshape(1, 16))
    gidxT = ct(np.asarray(L1_ffn_gate_idx, np.int32).T)
    uidxT = ct(np.asarray(L1_ffn_up_idx, np.int32).T)
    didxT = ct(np.asarray(L1_ffn_down_idx, np.int32).T)
    didxT_pad = np.zeros((6144, D), np.int32)
    didxT_pad[:INTER] = didxT

    resA = _run(_get("A", build_l0qkv), [
        {"x": ct(x[PERMS[c]]), "wqkvT": wqkvT,
         "wupT_sh": wupT[256 * c:256 * (c + 1)],
         "wgT_sh": wgT[256 * c:256 * (c + 1)],
         "wdT_sh": wdT[1024 * c:1024 * (c + 1)]} for c in range(NCORES)])
    kTf = np.empty((D, S), dtype=ml_dtypes.bfloat16)
    vhm_full = np.empty((S, H * 129), dtype=ml_dtypes.bfloat16)
    for c in range(NCORES):
        kTf[:, PERMS[c]] = np.asarray(resA[c]["kT"])
        vhm_full[PERMS[c]] = np.asarray(resA[c]["vhm"])
    vhmf = ct(vhm_full.reshape(S, H, 129).transpose(1, 0, 2))
    absw_all = np.stack([np.asarray(resA[c]["absw"])[0]
                         for c in range(NCORES)])

    resB = _run(_get("B", build_l0attn), [
        {"x": ct(x[PERMS[c]]), "qT": ct(np.asarray(resA[c]["qT"])),
         "kTf": kTf, "vhmf": vhmf, "woT": woT} for c in range(NCORES)])
    x1 = [ct(np.asarray(resB[c]["x1"])) for c in range(NCORES)]

    resC = _run(_get("C", build_l0quant), [
        {"absw_all": absw_all,
         "wupT_sh": wupT[256 * c:256 * (c + 1)],
         "wgT_sh": wgT[256 * c:256 * (c + 1)],
         "wdT_sh": wdT[1024 * c:1024 * (c + 1)]} for c in range(NCORES)])
    tupT = np.concatenate([np.asarray(resC[c]["tupT_sh"])
                           for c in range(NCORES)], 0)
    tgT = np.concatenate([np.asarray(resC[c]["tgT_sh"])
                          for c in range(NCORES)], 0)
    tdT = np.concatenate([np.asarray(resC[c]["tdT_sh"])
                          for c in range(NCORES)], 0)
    scl = np.asarray(resC[0]["scl"])

    resD = _run(_get("D", build_l0ffn), [
        {"x1": x1[c], "tupT": tupT, "tgT": tgT, "tdT": tdT, "scl": scl}
        for c in range(NCORES)])
    x2 = [ct(np.asarray(resD[c]["x2"])) for c in range(NCORES)]

    resE = _run(_get("E", build_l1prep), [
        {"x2": x2[c], "wdqT": ct(Wdq.T), "wuqT": ct(Wuq.T),
         "wdkvT": ct(Wdkv.T), "wuq": ct(Wuq), "wdq": ct(Wdq),
         "wuk": Wuk, "wuv": Wuv, "woT2": ct(Wo.T)} for c in range(NCORES)])
    qT2f = np.empty((D, S), dtype=ml_dtypes.bfloat16)
    ckvTf = np.empty((LKV, S), dtype=ml_dtypes.bfloat16)
    ckvaf = np.empty((S, 513), dtype=ml_dtypes.bfloat16)
    for c in range(NCORES):
        qT2f[:, PERMS[c]] = np.asarray(resE[c]["qT2"])
        ckvTf[:, PERMS[c]] = np.asarray(resE[c]["ckvT"])
        ckvaf[PERMS[c]] = np.asarray(resE[c]["ckva"])
    keff_full = np.asarray(resE[0]["keff"])
    veff_full = np.asarray(resE[0]["veff"])

    resG = _run(_get("G", build_l1mat), [
        {"lut": lut,
         "gidxT_sh": gidxT[256 * c:256 * (c + 1)],
         "uidxT_sh": uidxT[256 * c:256 * (c + 1)],
         "didxT_sh": didxT_pad[768 * c:768 * (c + 1)]} for c in range(NCORES)])
    wgT2 = np.concatenate([np.asarray(resG[c]["wgT_sh"])
                           for c in range(NCORES)], 0)
    wuT2 = np.concatenate([np.asarray(resG[c]["wuT_sh"])
                           for c in range(NCORES)], 0)
    wdT2 = np.concatenate([np.asarray(resG[c]["wdT2_sh"])
                           for c in range(NCORES)], 0)[:INTER]
    wdT2 = ct(wdT2)

    resF = _run(_get("F", build_l1attn), [
        {"qT2h": ct(qT2f[256 * c:256 * (c + 1)]),
         "ckvTf": ckvTf, "ckvaf": ckvaf,
         "keffh": ct(keff_full[256 * c:256 * (c + 1)]),
         "veffh": ct(veff_full[:, 256 * c:256 * (c + 1)])}
        for c in range(NCORES)])
    y_full = np.empty((S, D), dtype=ml_dtypes.bfloat16)
    for c in range(NCORES):
        y_full[:, 256 * c:256 * (c + 1)] = np.asarray(resF[c]["yh"])

    resH = _run(_get("H", build_l1postffn), [
        {"x2": x2[c], "yl": ct(y_full[PERMS[c]]),
         "wgT2": wgT2, "wuT2": wuT2, "wdT2": wdT2} for c in range(NCORES)])
    out = np.empty((S, D), np.float32)
    for c in range(NCORES):
        out[PERMS[c]] = np.asarray(resH[c]["x4"])
    return out[None]
